# version 06
# baseline (speedup 1.0000x reference)
import contextlib
import numpy as np
import ml_dtypes
import concourse.bass as bass
import concourse.mybir as mybir
from concourse.bass_utils import run_bass_kernel_spmd

F32 = mybir.dt.float32
BF16 = mybir.dt.bfloat16
AF = mybir.ActivationFunctionType
ALU = mybir.AluOpType
AX = mybir.AxisListType

D, H, HD, N = 1024, 16, 64, 4096
NT = 32
EPS = 1e-5
HS, HR, NE = 2048, 1024, 8
NCORES = 8
OB_W = 8 * N
_UPTO = 5
_PH4STAGE = 3
_NQ = 8
_ZMODE = 4
_NONORM = False
_NORMLVL = 6
_DUMP = None


class Toks:
    ENG = ("T", "S", "V", "G", "Y")

    def __init__(self, nc, stack):
        self.sem = {e: stack.enter_context(nc.semaphore("tok" + e)) for e in self.ENG}
        self.cnt = {e: 0 for e in self.ENG}

    def mark(self, inst, e):
        self.cnt[e] += 1
        inst.then_inc(self.sem[e], 1)
        return (self.sem[e], self.cnt[e])

    def cur(self, e):
        return (self.sem[e], self.cnt[e])

    @staticmethod
    def wait(eng, tok):
        if tok is not None and tok[1] > 0:
            eng.wait_ge(tok[0], tok[1])

    def fence(self, eng, inst, e):
        t = self.mark(inst, e)
        self.wait(eng, t)
        return t


class Dma:
    def __init__(self, nc, stack, name):
        self.sem = stack.enter_context(nc.semaphore("dma_" + name))
        self.cnt = 0

    def start(self, eng, dst, src):
        eng.dma_start(dst, src).then_inc(self.sem, 16)
        self.cnt += 16
        return (self.sem, self.cnt)

    @staticmethod
    def wait(eng, tok):
        if tok is not None:
            eng.wait_ge(tok[0], tok[1])


def build_program():
    nc = bass.Bass("TRN2", target_bir_lowering=False)
    st = contextlib.ExitStack()
    dtb = {}

    def dram(name, shape, dtyp=F32, kind="ExternalInput"):
        dtb[name] = nc.dram_tensor(name, shape, dtyp, kind=kind)
        return dtb[name]

    xb = dram("xb", [N, D])
    out = dram("out", [N, D], F32, "ExternalOutput")
    for nm in ("wqt", "wkt", "wvt"):
        dram(nm, [8, 128, 1024])
    dram("wot", [128, 8 * 1024])
    for nm in ("bq_r", "bk_r", "bv_r", "bo_r"):
        dram(nm, [1, D])
    dram("g8", [128, 8]); dram("b8", [128, 8])
    dram("cosA", [128, 1024], BF16); dram("sinA", [128, 1024], BF16)
    dram("id16", [128, 128], BF16); dram("id32", [128, 128], F32)
    dram("sw1", [16, 128, 1024]); dram("sw3", [16, 128, 1024])
    dram("sw2", [8, 128, 2048])
    dram("w1e", [NE, 128, 8 * 1024]); dram("w2e", [NE, 8, 128, 1024])
    dram("sb1_16", [128, 16]); dram("sb3_16", [128, 16]); dram("sb2_8", [128, 8])
    dram("b1_64", [128, 64]); dram("b2_64", [128, 64])
    fsh = dram("fsh", [D, N], BF16, "Internal")

    tk = Toks(nc, st)
    d_x = Dma(nc, st, "x")
    d_w = Dma(nc, st, "w")
    d_f = Dma(nc, st, "f")
    d_o = Dma(nc, st, "o")

    def SB(stack, name, shape, dtyp, side=None):
        return stack.enter_context(nc.sbuf_tensor(name, shape, dtyp, side=side))

    c_id16 = SB(st, "c_id16", [128, 128], BF16)
    c_id32 = SB(st, "c_id32", [128, 128], F32)
    c_cos = SB(st, "c_cos", [128, 1024], BF16)
    c_sin = SB(st, "c_sin", [128, 1024], BF16)
    c_g8 = SB(st, "c_g8", [128, 8], F32)
    c_b8 = SB(st, "c_b8", [128, 8], F32)
    c_bqr = SB(st, "c_bqr", [1, D], BF16)
    c_bkr = SB(st, "c_bkr", [1, D], BF16)
    c_bvr = SB(st, "c_bvr", [1, D], BF16)
    c_bor = SB(st, "c_bor", [1, D], BF16)
    c_sb1 = SB(st, "c_sb1", [128, 16], F32)
    c_sb3 = SB(st, "c_sb3", [128, 16], F32)
    c_sb2 = SB(st, "c_sb2", [128, 8], F32)
    c_b1 = SB(st, "c_b1", [128, 64], F32)
    c_b2 = SB(st, "c_b2", [128, 64], F32)
    c_shb = SB(st, "c_shb", [128, 64], F32)
    c_ones1 = SB(st, "c_ones1", [1, 128], BF16)
    c_ones128 = SB(st, "c_ones128", [128, 1], BF16)
    stat = SB(st, "stat", [128, 16], F32)

    with nc.Block("consts") as b:
        def _y(sync):
            t = None
            for nm, dst in (("id16", c_id16), ("id32", c_id32), ("cosA", c_cos),
                            ("sinA", c_sin), ("g8", c_g8), ("b8", c_b8),
                            ("sb1_16", c_sb1), ("sb3_16", c_sb3),
                            ("sb2_8", c_sb2), ("b1_64", c_b1), ("b2_64", c_b2)):
                t = d_x.start(sync, dst[:], dtb[nm].ap())
            d_x.wait(sync, t)
        b.sync(_y)

        def _g(g):
            t = None
            for nm, dst in (("bq_r", c_bqr), ("bk_r", c_bkr), ("bv_r", c_bvr),
                            ("bo_r", c_bor)):
                t = d_w.start(g, dst[:], dtb[nm].ap())
            d_w.wait(g, t)
        b.gpsimd(_g)

        def _v(v):
            Dma.wait(v, (d_x.sem, d_x.cnt))
            Dma.wait(v, (d_w.sem, d_w.cnt))
            v.memset(c_ones1[:], 1.0)
            m = v.memset(c_ones128[:], 1.0)
            t = tk.mark(m, "V")
            tk.wait(v, t)
            for r in range(8):
                e0, e1 = (3 * r) % 8, (3 * r + 7) % 8
                sl = c_shb[:, r * 8:(r + 1) * 8]
                m = v.tensor_add(sl, c_b2[:, e0 * 8:(e0 + 1) * 8],
                                 c_b2[:, e1 * 8:(e1 + 1) * 8])
                tk.fence(v, m, "V")
                m = v.tensor_scalar_mul(sl, sl, 0.25)
                tk.fence(v, m, "V")
                v.scalar_tensor_tensor(sl, c_sb2[:], 0.5, sl, ALU.mult, ALU.add)
        b.vector(_v)

    def emit_stats(b, src, width, m_col, nm_col, ss_col, sq_out):
        t1 = [None]

        def _v(v):
            m = v.reduce_sum(m_col, src[:, 0:width], axis=AX.X)
            tk.fence(v, m, "V")
            m = v.tensor_scalar_mul(m_col, m_col, 1.0 / width)
            tk.fence(v, m, "V")
            m = v.tensor_scalar_sub(sq_out[:, 0:width], src[:, 0:width], m_col)
            tk.fence(v, m, "V")
            m = v.scalar_tensor_tensor(sq_out[:, 0:width], sq_out[:, 0:width],
                                       1.0, sq_out[:, 0:width], ALU.mult,
                                       ALU.mult, accum_out=ss_col)
            t1[0] = tk.mark(m, "V")
        b.vector(_v)
        return t1[0]

    def emit_rs(b, ss_col, rs_col, width, wait_tok, combined=False):
        va, vb = stat[:, 12:13], stat[:, 13:14]
        t1 = [None]

        def _v1(v):
            tk.wait(v, wait_tok)
            m = v.tensor_scalar_mul(va, ss_col, 1.0 / width)
            tk.fence(v, m, "V")
            m = v.tensor_scalar_add(vb, va, EPS)
            tk.fence(v, m, "V")
            m = v.reciprocal(rs_col, vb)
            if combined:
                tk.fence(v, m, "V")
                m = v.tensor_mul(va, va, rs_col)
                tk.fence(v, m, "V")
                m = v.tensor_scalar_add(va, va, EPS)
                tk.fence(v, m, "V")
                m = v.reciprocal(vb, va)
                tk.fence(v, m, "V")
                m = v.tensor_mul(rs_col, rs_col, vb)
            t1[0] = tk.mark(m, "V")
        b.vector(_v1)
        tout = [None]

        def _s(s):
            tk.wait(s, t1[0])
            m = s.activation(rs_col, rs_col, AF.Sqrt)
            tout[0] = tk.mark(m, "S")
        b.scalar(_s)
        return tout[0]

    def x_tile_ap(tt):
        return xb.ap()[tt * 128:(tt + 1) * 128, :]

    def x_resmaj_ap(rt):
        r, j0 = rt // 4, (rt % 4) * 128
        return bass.AP(xb, (8 * j0 + r) * D, [[8 * D, 128], [1, D]])

    def out_resmaj_ap(rt):
        r, j0 = rt // 4, (rt % 4) * 128
        return bass.AP(out, (8 * j0 + r) * D, [[8 * D, 128], [1, D]])

    def emit_dump(aps):
        with nc.Block("dump") as bb:
            def _g(g):
                off = 0
                t = None
                for ap, nel in aps:
                    dst = bass.AP(out, off, [[nel // 128, 128], [1, nel // 128]])
                    t = d_o.start(g, dst, ap)
                    off += nel
                d_o.wait(g, t)
            bb.gpsimd(_g)

    sWB = contextlib.ExitStack()
    w_B = SB(sWB, "w_B", [128, 8 * N], BF16)

    with contextlib.ExitStack() as s1:
        xt0 = SB(s1, "xt0", [128, 1024], F32)
        xt1 = SB(s1, "xt1", [128, 1024], F32)
        appb = SB(s1, "appb1", [128, 1024], BF16)
        sqo = SB(s1, "sqo1", [128, 1024], F32)
        with nc.psum_tensor("p1t", [128, 1024], BF16) as p1t, \
                nc.Block("wbuild") as b:
            apply_tok = [None, None]
            evac_tok = [None]
            for tt in range(NT if _UPTO >= 1 else 0):
                src = xt0 if tt % 2 == 0 else xt1
                dtok = [None]

                def _y(sync):
                    if tt >= 2:
                        tk.wait(sync, apply_tok[tt % 2])
                    dtok[0] = d_x.start(sync, src[:], x_tile_ap(tt))
                b.sync(_y)

                def _v0(v):
                    Dma.wait(v, dtok[0])
                b.vector(_v0)
                sq_tok = emit_stats(b, src, 1024, stat[:, 0:1], stat[:, 1:2],
                                    stat[:, 2:3], sqo)
                rs_tok = emit_rs(b, stat[:, 2:3], stat[:, 3:4], 1024, sq_tok,
                                 combined=True)

                def _v1(v):
                    tk.wait(v, rs_tok)
                    m = v.tensor_scalar(appb[:], src[:], stat[:, 0:1],
                                        stat[:, 3:4], ALU.subtract, ALU.mult)
                    apply_tok[tt % 2] = tk.mark(m, "V")
                b.vector(_v1)
                trt = [None]

                def _t(t):
                    tk.wait(t, apply_tok[tt % 2])
                    tk.wait(t, evac_tok[0])
                    for c in range(8):
                        m = t.transpose(p1t[:, c * 128:(c + 1) * 128],
                                        appb[:, c * 128:(c + 1) * 128], c_id16[:])
                    trt[0] = tk.mark(m, "T")
                b.tensor(_t)

                def _s(s):
                    tk.wait(s, trt[0])
                    for c in range(8):
                        m = s.activation(
                            w_B[:, c * N + tt * 128: c * N + (tt + 1) * 128],
                            p1t[:, c * 128:(c + 1) * 128], AF.Identity,
                            bias=c_b8[:, c:c + 1], scale=c_g8[:, c:c + 1])
                    evac_tok[0] = tk.mark(m, "S")
                b.scalar(_s)

    if _DUMP == "wB":
        emit_dump([(w_B[:], 128 * 8 * N)])

    sOB = contextlib.ExitStack()
    o_B = SB(sOB, "o_B", [128, OB_W], BF16, side="right")

    with contextlib.ExitStack() as s2:
        qraw = SB(s2, "qraw", [128, N], BF16)
        kraw = SB(s2, "kraw", [128, N], BF16)
        q_B = SB(s2, "q_B", [128, N], BF16)
        k_B = SB(s2, "k_B", [128, N], BF16)
        v_A = SB(s2, "v_A", [128, 32 * 128], BF16)
        P0 = SB(s2, "P0", [128, 1024], BF16)
        P1 = SB(s2, "P1", [128, 1024], BF16)
        P2 = SB(s2, "P2", [128, 1024], BF16)
        P3 = SB(s2, "P3", [128, 1024], BF16)
        mr0, mr1, mr2 = P1, P2, P3
        oae = SB(s2, "oae", [128, 512], BF16)
        oas = SB(s2, "oas", [128, 512], BF16)
        rzb = SB(s2, "rzb", [128, 16], F32)
        zsb = SB(s2, "zsb", [128, 512], F32)
        wqs = SB(s2, "wqs", [128, 1024], BF16)
        wks = SB(s2, "wks", [128, 1024], BF16)
        wvs = SB(s2, "wvs", [128, 1024], BF16)
        cb = SB(s2, "cb", [128, 1], F32)
        cbs = SB(s2, "cbs", [1, 4], F32)
        cbb = SB(s2, "cbb", [1, 1], BF16)
        Ps = (P0, P1, P2, P3)

        prev_proj_tok = [None]

        npair = 8 if _UPTO >= 2 else 0
        if _DUMP in ("qkv", "oB"):
            npair = 1
        for pair in range(npair):
            with nc.Block("pair%d" % pair) as b:
                wtok = [None]

                def _g(g):
                    tk.wait(g, prev_proj_tok[0])
                    t = None
                    for nm, dst in (("wqt", wqs), ("wkt", wks), ("wvt", wvs)):
                        t = d_w.start(g, dst[:], dtb[nm].ap()[pair])
                    wtok[0] = t
                b.gpsimd(_g)

                with (nc.psum_tensor("pq%d" % pair, [128, 512], F32) as pq,
                      nc.psum_tensor("pk%d" % pair, [128, 512], F32) as pk,
                      nc.psum_tensor("pv%d" % pair, [128, 512], F32) as pv,
                      nc.psum_tensor("pq2%d" % pair, [128, 512], F32) as pq2,
                      nc.psum_tensor("pk2%d" % pair, [128, 512], F32) as pk2,
                      nc.psum_tensor("pv2%d" % pair, [128, 512], F32) as pv2):
                    pqs, pks, pvs = (pq, pq2), (pk, pk2), (pv, pv2)
                    ev_tok = [None, None]
                    mm_tok = [None, None]
                    for tg in range(8):
                        def _t(t):
                            if tg == 0:
                                Dma.wait(t, wtok[0])
                            if tg >= 2:
                                tk.wait(t, ev_tok[tg % 2])
                            A, Bm, C = pqs[tg % 2], pks[tg % 2], pvs[tg % 2]
                            for j in range(4):
                                tt = tg * 4 + j
                                for kc in range(8):
                                    sw = w_B[:, kc * N + tt * 128:
                                             kc * N + (tt + 1) * 128]
                                    st0 = (kc == 0)
                                    t.matmul(A[:, j * 128:(j + 1) * 128], sw,
                                             wqs[:, kc * 128:(kc + 1) * 128],
                                             start=st0, stop=False)
                                    t.matmul(Bm[:, j * 128:(j + 1) * 128], sw,
                                             wks[:, kc * 128:(kc + 1) * 128],
                                             start=st0, stop=False)
                                    t.matmul(C[:, j * 128:(j + 1) * 128], sw,
                                             wvs[:, kc * 128:(kc + 1) * 128],
                                             start=st0, stop=False)
                                t.matmul(A[:, j * 128:(j + 1) * 128], c_ones1[:],
                                         c_bqr[:, pair * 128:(pair + 1) * 128],
                                         start=False, stop=True)
                                t.matmul(Bm[:, j * 128:(j + 1) * 128], c_ones1[:],
                                         c_bkr[:, pair * 128:(pair + 1) * 128],
                                         start=False, stop=True)
                                m = t.matmul(C[:, j * 128:(j + 1) * 128],
                                             c_ones1[:],
                                             c_bvr[:, pair * 128:(pair + 1) * 128],
                                             start=False, stop=True)
                            mm_tok[tg % 2] = tk.mark(m, "T")
                            if tg == 7:
                                prev_proj_tok[0] = mm_tok[tg % 2]
                        b.tensor(_t)

                        def _v(v):
                            tk.wait(v, mm_tok[tg % 2])
                            A, Bm, C = pqs[tg % 2], pks[tg % 2], pvs[tg % 2]
                            v.tensor_copy(qraw[:, tg * 512:(tg + 1) * 512], A[:])
                            v.tensor_copy(kraw[:, tg * 512:(tg + 1) * 512], Bm[:])
                            m = v.tensor_copy(v_A[:, tg * 512:(tg + 1) * 512],
                                              C[:])
                            ev_tok[tg % 2] = tk.mark(m, "V")
                        b.vector(_v)

                rope_tok = [None]

                def _vo(v):
                    tk.wait(v, ev_tok[0])
                    tk.wait(v, ev_tok[1])
                    for raw in (qraw, kraw):
                        for hf in range(2):
                            o0 = hf * 16 * 128
                            oc = hf * 16 * 32
                            tops = bass.AP(raw, o0, [[N, 128], [128, 16], [64, 2], [1, 32]])
                            bots = bass.AP(raw, o0 + 32, [[N, 128], [128, 16], [64, 2], [1, 32]])
                            cosr = bass.AP(c_cos, oc, [[1024, 128], [32, 16], [0, 2], [1, 32]])
                            sinr = bass.AP(c_sin, oc, [[1024, 128], [32, 16], [0, 2], [1, 32]])
                            m = v.tensor_tensor(mr0[:], tops, cosr, ALU.mult)
                            v.tensor_tensor(mr1[:], bots, sinr, ALU.mult)
                            v.tensor_tensor(mr2[:], tops, sinr, ALU.mult)
                            v.tensor_tensor(tops, mr0[:], mr1[:], ALU.subtract)
                            v.tensor_tensor(mr0[:], bots, cosr, ALU.mult)
                            m = v.tensor_tensor(bots, mr2[:], mr0[:], ALU.add)
                    rope_tok[0] = tk.mark(m, "V")
                b.vector(_vo)

                cb_tok = [None]
                with nc.psum_tensor("p2t%d" % pair, [128, 1024], BF16) as p2t:
                    tb_ev = [None]
                    for raw, dstB in ((qraw, q_B), (kraw, k_B)):
                        for grp in range(4):
                            trt = [None]

                            def _t(t, raw=raw, grp=grp):
                                tk.wait(t, rope_tok[0])
                                tk.wait(t, tb_ev[0])
                                for c in range(8):
                                    tt = grp * 8 + c
                                    m = t.transpose(p2t[:, c * 128:(c + 1) * 128],
                                                    raw[:, tt * 128:(tt + 1) * 128],
                                                    c_id16[:])
                                trt[0] = tk.mark(m, "T")
                            b.tensor(_t)

                            def _v(v, dstB=dstB, grp=grp):
                                tk.wait(v, trt[0])
                                m = v.tensor_copy(
                                    dstB[:, grp * 1024:(grp + 1) * 1024], p2t[:])
                                tb_ev[0] = tk.mark(m, "V")
                            b.vector(_v)

                with nc.psum_tensor("pn%d" % pair, [128, 4096], F32) as pn:
                    sq_tok = [None]

                    def _vsq(v):
                        tk.wait(v, tb_ev[0])
                        v.tensor_mul(qraw[:, 0:2048], q_B[:, 0:2048],
                                     q_B[:, 0:2048])
                        v.tensor_mul(qraw[:, 2048:4096], q_B[:, 2048:4096],
                                     q_B[:, 2048:4096])
                        v.tensor_mul(kraw[:, 0:2048], k_B[:, 0:2048],
                                     k_B[:, 0:2048])
                        m = v.tensor_mul(kraw[:, 2048:4096],
                                         k_B[:, 2048:4096],
                                         k_B[:, 2048:4096])
                        sq_tok[0] = tk.mark(m, "V")
                    b.vector(_vsq)
                    nm_tok = [None]

                    def _tn(t):
                        tk.wait(t, sq_tok[0])
                        for j, buf in enumerate((qraw, kraw)):
                            for ch in range(4):
                                m = t.matmul(
                                    pn[0:1, (j * 4 + ch) * 512:
                                       (j * 4 + ch + 1) * 512],
                                    c_ones128[:],
                                    bass.AP(buf, ch * 1024,
                                            [[4096, 128], [512, 2], [1, 512]])
                                    if False else
                                    buf[:, ch * 1024: ch * 1024 + 512],
                                    start=True, stop=True)
                        for j, buf in enumerate((qraw, kraw)):
                            for ch in range(4):
                                m = t.matmul(
                                    pn[0:1, (j * 4 + ch) * 512:
                                       (j * 4 + ch + 1) * 512],
                                    c_ones128[:],
                                    buf[:, (ch + 4) * 512 + 0:0] if False else
                                    buf[:, 2048 + ch * 512: 2048 + (ch + 1) * 512],
                                    start=False, stop=True)
                        nm_tok[0] = tk.mark(m, "T")
                    b.tensor(_tn)
                    ct3 = [None]

                    def _vn(v):
                        tk.wait(v, nm_tok[0])
                        m = v.reduce_max(cbs[0:1, 0:1], pn[0:1, 0:2048],
                                         axis=AX.X)
                        tk.fence(v, m, "V")
                        m = v.reduce_max(cbs[0:1, 1:2], pn[0:1, 2048:4096],
                                         axis=AX.X)
                        tk.fence(v, m, "V")
                        m = v.tensor_mul(cbs[0:1, 2:3], cbs[0:1, 0:1],
                                         cbs[0:1, 1:2])
                        ct3[0] = tk.mark(m, "V")
                    b.vector(_vn)
                    ct4 = [None]

                    def _sc(s):
                        tk.wait(s, ct3[0])
                        m = s.activation(cbs[0:1, 3:4], cbs[0:1, 2:3], AF.Sqrt)
                        ct4[0] = tk.mark(m, "S")
                    b.scalar(_sc)
                    ct5 = [None]

                    def _vc2(v):
                        tk.wait(v, ct4[0])
                        m = v.tensor_scalar_mul(cbs[0:1, 3:4], cbs[0:1, 3:4],
                                                -0.125)
                        tk.fence(v, m, "V")
                        m = v.tensor_copy(cbb[:], cbs[0:1, 3:4])
                        ct5[0] = tk.mark(m, "V")
                    b.vector(_vc2)
                    ct6 = [None]

                    def _tc(t):
                        tk.wait(t, ct5[0])
                        m = t.matmul(pn[:, 0:1], c_ones1[:], cbb[:],
                                     start=True, stop=True)
                        ct6[0] = tk.mark(m, "T")
                    b.tensor(_tc)

                    def _vc3(v):
                        tk.wait(v, ct6[0])
                        m = v.tensor_copy(cb[:], pn[:, 0:1])
                        cb_tok[0] = tk.mark(m, "V")
                    b.vector(_vc3)

                with (nc.psum_tensor("sc0%d" % pair, [128, 1024], F32) as sc0,
                      nc.psum_tensor("sc1%d" % pair, [128, 1024], F32) as sc1,
                      nc.psum_tensor("pvAB%d" % pair, [128, 512], F32) as pvAB,
                      nc.psum_tensor("zps%d" % pair, [128, 512], F32) as zps,
                      nc.psum_tensor("tpbA%d" % pair, [128, 512], BF16) as tpbA,
                      nc.psum_tensor("tpbB%d" % pair, [128, 512], BF16) as tpbB):
                    scs = (sc0, sc1)
                    semT, semS = tk.sem["T"], tk.sem["S"]
                    ev_t = [None]
                    ft_t = [None]
                    rz_t = [None]
                    scale_tok = [None]
                    bt_t = [None]
                    obe_tok = [None]
                    prev_cnt = [None]
                    ORDER = [("sc", 0), ("sc", 1)]
                    for p in range(16):
                        ORDER.append(("pv", 2 * p))
                        if 2 * p + 2 < 32:
                            ORDER.append(("sc", 2 * p + 2))
                        ORDER.append(("pv", 2 * p + 1))
                        if 2 * p + 3 < 32:
                            ORDER.append(("sc", 2 * p + 3))
                        ORDER.append(("z", p))

                    def emit_evac(pvc):
                        def _ve(v):
                            v.wait_ge(semT, pvc[("z", 15)])
                            tk.wait(v, ft_t[0])
                            v.tensor_copy(oae[:, 0:512], pvAB[:])
                            for rr in (0, 32, 64, 96):
                                m = v.tensor_copy(zsb[rr:rr + 1, :],
                                                  zps[rr:rr + 1, :])
                            ev_t[0] = tk.mark(m, "V")
                        b.vector(_ve)
                        if _NORMLVL < 2:
                            return

                        def _tf(t):
                            tk.wait(t, ev_t[0])
                            tk.wait(t, obe_tok[0])
                            m = None
                            if _NORMLVL != 16:
                                js = (range(4) if _NORMLVL == 17 else
                                      range(4, 8) if _NORMLVL == 18 else
                                      range(8))
                                for j in js:
                                    h, jj = (0, j) if j < 4 else (1, j - 4)
                                    src_ap = bass.AP(oae,
                                                     h * 64 * 512 + jj * 128,
                                                     [[512, 64], [1, 128]])
                                    ident = (c_id16[0:64, 0:64] if h == 0
                                             else c_id16[64:128, 64:128])
                                    tdst = tpbA if h == 0 else tpbB
                                    m = t.transpose(
                                        tdst[:, jj * 64:(jj + 1) * 64],
                                        src_ap, ident,
                                        tile_position=(64 * h, 0))
                            if _NORMLVL != 15:
                                for jj in range(4):
                                    m = t.transpose(
                                        bass.AP(pvAB, jj * 128,
                                                [[512, 128], [1, 128]]),
                                        bass.AP(zsb, jj * 128,
                                                [[512, 128], [1, 128]]),
                                        c_id32[:])
                            ft_t[0] = tk.mark(m, "T")
                        b.tensor(_tf)
                        if _NORMLVL < 3 or _NORMLVL >= 15:
                            return

                        def _vr(v):
                            tk.wait(v, ft_t[0])
                            tk.wait(v, scale_tok[0])
                            m = v.tensor_copy(
                                rzb[:, 8:12],
                                bass.AP(pvAB, 0, [[512, 128], [128, 4], [1, 1]]))
                            tk.fence(v, m, "V")
                            m = v.scalar_tensor_tensor(
                                rzb[:, 8:12],
                                bass.AP(pvAB, 64, [[512, 128], [128, 4], [1, 1]]),
                                1.0, rzb[:, 8:12], ALU.mult, ALU.add)
                            tk.fence(v, m, "V")
                            m = v.tensor_copy(
                                rzb[:, 12:16],
                                bass.AP(pvAB, 32, [[512, 128], [128, 4], [1, 1]]))
                            tk.fence(v, m, "V")
                            m = v.scalar_tensor_tensor(
                                rzb[:, 12:16],
                                bass.AP(pvAB, 96, [[512, 128], [128, 4], [1, 1]]),
                                1.0, rzb[:, 12:16], ALU.mult, ALU.add)
                            tk.fence(v, m, "V")
                            v.reciprocal(rzb[:, 0:4], rzb[:, 8:12])
                            m = v.reciprocal(rzb[:, 4:8], rzb[:, 12:16])
                            rz_t[0] = tk.mark(m, "V")
                        b.vector(_vr)

                    def emit_qnorm(Qn):
                        if _NORMLVL < 4:
                            return
                        def _ss(s):
                            tk.wait(s, rz_t[0])
                            tk.wait(s, bt_t[0])
                            for j in range(8):
                                h, jj = (0, j) if j < 4 else (1, j - 4)
                                tsrc = tpbA if h == 0 else tpbB
                                m = s.activation(oas[:, j * 64:(j + 1) * 64],
                                                 tsrc[:, jj * 64:
                                                      (jj + 1) * 64],
                                                 AF.Copy,
                                                 scale=rzb[:, h * 4 + jj:
                                                           h * 4 + jj + 1])
                            scale_tok[0] = tk.mark(m, "S")
                        b.scalar(_ss)
                        if _NORMLVL < 5:
                            return

                        def _tb(t):
                            tk.wait(t, scale_tok[0])
                            for j in range(8):
                                tdst = tpbA if j < 4 else tpbB
                                m = t.transpose(tdst[0:64, (j % 4) * 128:
                                                     (j % 4) * 128 + 128],
                                                oas[:, j * 64:(j + 1) * 64],
                                                c_id16[:])
                            bt_t[0] = tk.mark(m, "T")
                        b.tensor(_tb)
                        if _NORMLVL < 6:
                            return

                        def _vb(v, Qn=Qn, pair=pair):
                            tk.wait(v, bt_t[0])
                            m = None
                            for j in range(8):
                                head = 0 if j < 4 else 1
                                jj = j % 4
                                dst = bass.AP(
                                    o_B,
                                    head * 64 * OB_W + pair * N + Qn * 64 + jj * 16,
                                    [[OB_W, 64], [1, 16], [512, 8]])
                                tsrc = tpbA if j < 4 else tpbB
                                m = v.tensor_copy(
                                    dst,
                                    bass.AP(tsrc, (j % 4) * 128,
                                            [[512, 64], [1, 128]]))
                            obe_tok[0] = tk.mark(m, "V")
                        b.vector(_vb)

                    for Q in range(_NQ):
                        if Q >= 1 and not _NONORM:
                            emit_evac(prev_cnt[0])

                        base_T = tk.cnt["T"]
                        base_S = tk.cnt["S"]
                        cnt = {}
                        c = base_T
                        for item in ORDER:
                            c += 1
                            cnt[item] = c
                        prev_q_cnt = prev_cnt[0]

                        def exp_cnt(kt, base_S=base_S):
                            return base_S + kt + 1

                        def S_body(s, Q=Q, cnt=cnt, prev_q_cnt=prev_q_cnt):
                            for kt in range(32):
                                s.wait_ge(semT, cnt[("sc", kt)])
                                if kt == 0 and Q == 0:
                                    tk.wait(s, cb_tok[0])
                                if kt < 4:
                                    if prev_q_cnt is not None:
                                        s.wait_ge(semT,
                                                  prev_q_cnt[("z", 14 + kt // 2)])
                                else:
                                    s.wait_ge(semT, cnt[("z", (kt - 4) // 2)])
                                m = s.activation(Ps[kt % 4][:], scs[kt % 2][:],
                                                 AF.Exp, bias=cb[:], scale=0.125)
                                got = tk.mark(m, "S")
                                assert got == (semS, exp_cnt(kt))
                        b.scalar(S_body)

                        def T_body(t, Q=Q, cnt=cnt):
                            def sc(kt):
                                buf = scs[kt % 2]
                                t.matmul(buf[:, 0:512],
                                         k_B[0:64, kt * 128:(kt + 1) * 128],
                                         q_B[0:64, Q * 512:(Q + 1) * 512],
                                         start=True, stop=True,
                                         tile_position=(0, 0))
                                m = t.matmul(buf[:, 512:1024],
                                             k_B[64:128, kt * 128:(kt + 1) * 128],
                                             q_B[64:128, Q * 512:(Q + 1) * 512],
                                             start=True, stop=True,
                                             tile_position=(64, 0))
                                got = tk.mark(m, "T")
                                assert got == (semT, cnt[("sc", kt)])

                            def pv(kt):
                                t.wait_ge(semS, exp_cnt(kt))
                                if kt == 0:
                                    tk.wait(t, ev_t[0])
                                    tk.wait(t, rz_t[0])
                                Pb = Ps[kt % 4]
                                t.matmul(pvAB[0:64, :],
                                         v_A[:, kt * 128: kt * 128 + 64],
                                         Pb[:, 0:512], start=(kt == 0),
                                         stop=(kt == 31), tile_position=(0, 0))
                                m = t.matmul(pvAB[64:128, :],
                                             v_A[:, kt * 128 + 64:
                                                 kt * 128 + 128],
                                             Pb[:, 512:1024], start=(kt == 0),
                                             stop=(kt == 31),
                                             tile_position=(0, 64))
                                got = tk.mark(m, "T")
                                assert got == (semT, cnt[("pv", kt)])

                            def zmm(p):
                                st0 = (p == 0)
                                if _ZMODE == 4:
                                    t.matmul(zps[0:1, :], c_ones128[:],
                                             Ps[(2 * p) % 4][:, 0:512],
                                             start=st0, stop=(p == 15),
                                             tile_position=(0, 0))
                                    t.matmul(zps[32:33, :], c_ones128[:],
                                             Ps[(2 * p) % 4][:, 512:1024],
                                             start=st0, stop=(p == 15),
                                             tile_position=(0, 32))
                                    t.matmul(zps[64:65, :], c_ones128[:],
                                             Ps[(2 * p + 1) % 4][:, 0:512],
                                             start=st0, stop=(p == 15),
                                             tile_position=(0, 64))
                                    m = t.matmul(zps[96:97, :], c_ones128[:],
                                                 Ps[(2 * p + 1) % 4][:, 512:1024],
                                                 start=st0, stop=(p == 15),
                                                 tile_position=(0, 96))
                                else:
                                    m = t.matmul(zps[0:1, :], c_ones128[:],
                                                 Ps[(2 * p) % 4][:, 0:512],
                                                 start=st0, stop=(p == 15))
                                got = tk.mark(m, "T")
                                assert got == (semT, cnt[("z", p)])

                            if Q == 0:
                                tk.wait(t, cb_tok[0])
                            for kind, idx in ORDER:
                                if kind == "sc":
                                    sc(idx)
                                elif kind == "pv":
                                    pv(idx)
                                else:
                                    zmm(idx)
                        b.tensor(T_body)
                        prev_cnt[0] = cnt

                        if Q >= 1 and not _NONORM:
                            emit_qnorm(Q - 1)

                    if not _NONORM:
                        emit_evac(prev_cnt[0])
                        emit_qnorm(_NQ - 1)

        if _DUMP == "qkv":
            emit_dump([(q_B[:], 128 * N), (k_B[:], 128 * N),
                       (v_A[:], 128 * 32 * 130), (cb[:], 128)])
        if _DUMP == "oB":
            emit_dump([(o_B[:, 0:N], 128 * N)])

    sWB.close()

    sLN = contextlib.ExitStack()
    ln3r = SB(sLN, "ln3r", [128, NT * 1024], BF16)

    with contextlib.ExitStack() as s3:
        wotb = SB(s3, "wotb", [128, 8 * 1024], BF16)
        xt0 = SB(s3, "xt0b", [128, 1024], F32)
        xt1 = SB(s3, "xt1b", [128, 1024], F32)
        app = SB(s3, "app3", [128, 1024], F32)
        sqo = SB(s3, "sqo3", [128, 1024], F32)
        wot_tok = [None]
        with nc.Block("woload") as b:
            def _g(g):
                wot_tok[0] = d_w.start(g, wotb[:], dtb["wot"].ap())
            b.gpsimd(_g)

        with (nc.psum_tensor("rp0", [128, 1024], F32) as rp0,
              nc.psum_tensor("rp1", [128, 1024], F32) as rp1,
              nc.Block("wo") as b):
            rps = (rp0, rp1)
            r_tok = [None, None]
            fin_tok = [None, None]
            for rt in range(NT if (_UPTO >= 3 and _DUMP not in ("wB", "qkv", "oB")) else 0):
                src = xt0 if rt % 2 == 0 else xt1
                dtok = [None]

                def _y(sync):
                    if rt >= 2:
                        tk.wait(sync, fin_tok[rt % 2])
                    dtok[0] = d_x.start(sync, src[:], x_resmaj_ap(rt))
                b.sync(_y)

                def _t(t):
                    if rt == 0:
                        Dma.wait(t, wot_tok[0])
                    if rt >= 2:
                        tk.wait(t, fin_tok[rt % 2])
                    R = rps[rt % 2]
                    m = None
                    for cch in range(2):
                        for kc in range(8):
                            t.matmul(R[:, cch * 512:(cch + 1) * 512],
                                     o_B[:, kc * N + rt * 128:
                                         kc * N + (rt + 1) * 128],
                                     wotb[:, kc * 1024 + cch * 512:
                                          kc * 1024 + (cch + 1) * 512],
                                     start=(kc == 0), stop=False)
                        m = t.matmul(R[:, cch * 512:(cch + 1) * 512], c_ones1[:],
                                     c_bor[:, cch * 512:(cch + 1) * 512],
                                     start=False, stop=True)
                    r_tok[rt % 2] = tk.mark(m, "T")
                b.tensor(_t)

                def _v0(v):
                    Dma.wait(v, dtok[0])
                b.vector(_v0)
                sq_tok = emit_stats(b, src, 1024, stat[:, 0:1], stat[:, 1:2],
                                    stat[:, 2:3], sqo)
                rs_tok = emit_rs(b, stat[:, 2:3], stat[:, 3:4], 1024, sq_tok)

                def _v1(v):
                    tk.wait(v, rs_tok)
                    tk.wait(v, r_tok[rt % 2])
                    v.tensor_scalar(app[:], src[:], stat[:, 0:1], stat[:, 3:4],
                                    ALU.subtract, ALU.mult)
                    m = v.tensor_add(app[:], app[:], rps[rt % 2][:])
                    tk.mark(m, "V")
                b.vector(_v1)
                sq2 = emit_stats(b, app, 1024, stat[:, 4:5], stat[:, 5:6],
                                 stat[:, 6:7], sqo)
                rs2 = emit_rs(b, stat[:, 6:7], stat[:, 7:8], 1024, sq2)

                def _v2(v):
                    tk.wait(v, rs2)
                    m = v.tensor_scalar(ln3r[:, rt * 1024:(rt + 1) * 1024],
                                        app[:], stat[:, 4:5], stat[:, 7:8],
                                        ALU.subtract, ALU.mult)
                    fin_tok[rt % 2] = tk.mark(m, "V")
                b.vector(_v2)

        if _DUMP == "ln3r":
            emit_dump([(ln3r[:], 128 * NT * 1024)])

    sOB.close()

    sU2 = contextlib.ExitStack()
    u2g = SB(sU2, "u2g", [128, 8 * N], BF16, side="right")

    with contextlib.ExitStack() as s4:
        hh = SB(s4, "hh", [128, 16 * 1024], BF16)
        sw1s = [SB(s4, "sw1s%d" % i, [128, 1024], BF16) for i in range(2)]
        sw3s = [SB(s4, "sw3s%d" % i, [128, 1024], BF16) for i in range(2)]
        sw2s = [SB(s4, "sw2s%d" % i, [128, 2048], BF16) for i in range(2)]
        ssl = SB(s4, "ssl", [128, 1024], BF16)
        xt0 = SB(s4, "xt0c", [128, 1024], F32)
        app = SB(s4, "app4", [128, 1024], F32)
        appb = SB(s4, "appb4", [128, 1024], BF16)
        sqo = xt0
        fsht = [SB(s4, "fsht%d" % i, [128, 512], BF16) for i in range(2)]

        for blk in range(4 if (_UPTO >= 4 and _DUMP not in ("wB", "qkv", "oB", "ln3r")) else 0):
            with nc.Block("sh%d" % blk) as b:
                with nc.psum_tensor("p4t%d" % blk, [128, 1024], BF16) as p4t:
                    ua_tok = [None]
                    ue_tok = [None]
                    trt_prev = [None]
                    for rtl in range(8):
                        rt = blk * 8 + rtl
                        dtok = [None]

                        def _y(sync, rt=rt):
                            tk.wait(sync, ua_tok[0])
                            dtok[0] = d_x.start(sync, xt0[:], x_resmaj_ap(rt))
                        b.sync(_y)
                        x2_tok = [None]

                        def _v0(v, rt=rt):
                            Dma.wait(v, dtok[0])
                            m = v.tensor_add(app[:], xt0[:],
                                             ln3r[:, rt * 1024:(rt + 1) * 1024])
                            x2_tok[0] = tk.mark(m, "V")
                        b.vector(_v0)
                        sq_tok = emit_stats(b, app, 1024, stat[:, 0:1],
                                            stat[:, 1:2], stat[:, 2:3], sqo)
                        rs_tok = emit_rs(b, stat[:, 2:3], stat[:, 3:4], 1024,
                                         sq_tok)

                        def _v1(v):
                            tk.wait(v, rs_tok)
                            tk.wait(v, trt_prev[0])
                            m = v.tensor_scalar(appb[:], app[:], stat[:, 0:1],
                                                stat[:, 3:4], ALU.subtract,
                                                ALU.mult)
                            ua_tok[0] = tk.mark(m, "V")
                        b.vector(_v1)
                        trt = [None]

                        def _t(t):
                            tk.wait(t, ua_tok[0])
                            tk.wait(t, ue_tok[0])
                            for c in range(8):
                                m = t.transpose(p4t[:, c * 128:(c + 1) * 128],
                                                appb[:, c * 128:(c + 1) * 128],
                                                c_id16[:])
                            trt[0] = tk.mark(m, "T")
                            trt_prev[0] = trt[0]
                        b.tensor(_t)

                        def _v2(v, rt=rt):
                            tk.wait(v, trt[0])
                            dst = bass.AP(u2g, rt * 128,
                                          [[8 * N, 128], [N, 8], [1, 128]])
                            m = v.tensor_copy(dst, p4t[:])
                            ue_tok[0] = tk.mark(m, "V")
                        b.vector(_v2)

                with (nc.psum_tensor("ph1_%d" % blk, [128, 512], F32) as ph1,
                      nc.psum_tensor("ph3_%d" % blk, [128, 512], F32) as ph3,
                      nc.psum_tensor("ph1b%d" % blk, [128, 512], F32) as ph1b,
                      nc.psum_tensor("ph3b%d" % blk, [128, 512], F32) as ph3b):
                    h1s, h3s = (ph1, ph1b), (ph3, ph3b)
                    wdt = [None, None]
                    hm_tok = [None, None]
                    hv_tok = [None, None]
                    sl_tok = [None]
                    for ht in range(16 if _PH4STAGE >= 2 else 0):
                        def _g(g, ht=ht):
                            tk.wait(g, hm_tok[0])
                            tk.wait(g, hm_tok[1])
                            d_w.start(g, sw1s[ht % 2][:], dtb["sw1"].ap()[ht])
                            wdt[ht % 2] = d_w.start(g, sw3s[ht % 2][:],
                                                    dtb["sw3"].ap()[ht])
                        b.gpsimd(_g)
                        for ch in range(2):
                            def _t(t, ht=ht, ch=ch, blk=blk):
                                Dma.wait(t, wdt[ht % 2])
                                if ht == 0 and ch == 0:
                                    tk.wait(t, ue_tok[0])
                                tk.wait(t, hv_tok[ch])
                                A, Bm = h1s[ch], h3s[ch]
                                lo = blk * 1024 + ch * 512
                                for kc in range(8):
                                    t.matmul(A[:],
                                             sw1s[ht % 2][:, kc * 128:(kc + 1) * 128],
                                             u2g[:, kc * N + lo: kc * N + lo + 512],
                                             start=(kc == 0), stop=(kc == 7))
                                for kc in range(8):
                                    m = t.matmul(
                                        Bm[:],
                                        sw3s[ht % 2][:, kc * 128:(kc + 1) * 128],
                                        u2g[:, kc * N + lo: kc * N + lo + 512],
                                        start=(kc == 0), stop=(kc == 7))
                                hm_tok[ch] = tk.mark(m, "T")
                            b.tensor(_t)

                            def _s(s, ht=ht, ch=ch):
                                tk.wait(s, hm_tok[ch])
                                tk.wait(s, hv_tok[ch])
                                m = s.activation(ssl[:, ch * 512:(ch + 1) * 512],
                                                 h1s[ch][:], AF.Silu,
                                                 bias=c_sb1[:, ht:ht + 1])
                                sl_tok[0] = tk.mark(m, "S")
                            b.scalar(_s)

                            def _v(v, ht=ht, ch=ch):
                                tk.wait(v, sl_tok[0])
                                m = v.scalar_tensor_tensor(
                                    hh[:, ht * 1024 + ch * 512:
                                       ht * 1024 + (ch + 1) * 512],
                                    h3s[ch][:], c_sb3[:, ht:ht + 1],
                                    ssl[:, ch * 512:(ch + 1) * 512],
                                    ALU.add, ALU.mult)
                                hv_tok[ch] = tk.mark(m, "V")
                            b.vector(_v)

                with (nc.psum_tensor("pC0_%d" % blk, [128, 512], F32) as pC0,
                      nc.psum_tensor("pC1_%d" % blk, [128, 512], F32) as pC1):
                    pcs = (pC0, pC1)
                    cdt = [None, None]
                    cm_tok = [None, None]
                    cs_tok = [None, None]
                    dfw = [None, None]
                    for ct in range(8 if _PH4STAGE >= 3 else 0):
                        def _g(g, ct=ct):
                            tk.wait(g, cm_tok[0])
                            tk.wait(g, cm_tok[1])
                            cdt[ct % 2] = d_w.start(g, sw2s[ct % 2][:],
                                                    dtb["sw2"].ap()[ct])
                        b.gpsimd(_g)
                        for ch in range(2):
                            idx = ch

                            def _t(t, ct=ct, ch=ch, blk=blk, idx=idx):
                                Dma.wait(t, cdt[ct % 2])
                                if ct == 0 and ch == 0:
                                    tk.wait(t, hv_tok[0])
                                    tk.wait(t, hv_tok[1])
                                tk.wait(t, cs_tok[idx])
                                m = None
                                for kc in range(16):
                                    m = t.matmul(
                                        pcs[idx][:],
                                        sw2s[ct % 2][:, kc * 128:(kc + 1) * 128],
                                        hh[:, kc * 1024 + ch * 512:
                                           kc * 1024 + (ch + 1) * 512],
                                        start=(kc == 0), stop=(kc == 15))
                                cm_tok[idx] = tk.mark(m, "T")
                            b.tensor(_t)

                            def _s(s, ct=ct, ch=ch, blk=blk, idx=idx):
                                tk.wait(s, cm_tok[idx])
                                Dma.wait(s, dfw[idx])
                                r = 2 * blk + ch
                                m = s.activation(fsht[idx][:], pcs[idx][:],
                                                 AF.Identity,
                                                 bias=c_shb[:, r * 8 + ct:
                                                            r * 8 + ct + 1],
                                                 scale=0.5)
                                t2 = tk.mark(m, "S")
                                tk.wait(s, t2)
                                dfw[idx] = d_f.start(
                                    s, fsh.ap()[ct * 128:(ct + 1) * 128,
                                                blk * 1024 + ch * 512:
                                                blk * 1024 + (ch + 1) * 512],
                                    fsht[idx][:])
                                cs_tok[idx] = t2
                            b.scalar(_s)

        if _DUMP == "u2g":
            emit_dump([(u2g[:], 128 * 8 * N)])
        if _DUMP == "fsh":
            emit_dump([(fsh.ap(), D * N)])

    with contextlib.ExitStack() as s5:
        w1b = SB(s5, "w1b", [128, 8 * 1024], BF16)
        w2s = [SB(s5, "w2s%d" % i, [128, 1024], BF16) for i in range(4)]
        he = [SB(s5, "he%d" % i, [128, 4096], BF16) for i in range(2)]
        fshs = SB(s5, "fshs", [128, 2048], BF16)
        fsc = SB(s5, "fsc", [128, 2048], F32)
        xt0 = SB(s5, "xt0d", [128, 1024], F32)
        app = SB(s5, "app5", [128, 1024], F32)
        sqo = app

        w1_tok = [None]
        fv_prev_g = [None]
        prev_w2 = [None, None]
        prev_ct = [None]
        prev_he = [None]
        prev_out = [None]
        with nc.Block("moepre") as b:
            def _g(g):
                w1_tok[0] = d_w.start(g, w1b[:], dtb["w1e"].ap()[0])
            b.gpsimd(_g)

        for r in range(8 if (_UPTO >= 5 and _DUMP in (None, "fsc")) else 0):
            e0, e1 = (3 * r) % 8, (3 * r + 7) % 8
            with nc.Block("moe%d" % r) as b:
                with (nc.psum_tensor("phe0_%d" % r, [128, 512], F32) as phe0,
                      nc.psum_tensor("phe1_%d" % r, [128, 512], F32) as phe1):
                    phe = (phe0, phe1)
                    ge_tok = [None, None]
                    he_mm = [None, None]
                    for hi, e in enumerate((e0, e1)):
                        for ht in range(8):
                            idx = ht % 2

                            def _t(t, hi=hi, ht=ht, r=r, idx=idx,
                                   pf_fence=None):
                                if ht == 0:
                                    Dma.wait(t, w1_tok[0])
                                if hi == 0 and ht == 0:
                                    tk.wait(t, prev_ct[0])
                                    tk.wait(t, pf_fence)
                                tk.wait(t, ge_tok[idx])
                                m = None
                                for kc in range(8):
                                    m = t.matmul(
                                        phe[idx][:],
                                        w1b[:, kc * 1024 + ht * 128:
                                            kc * 1024 + (ht + 1) * 128],
                                        u2g[:, kc * N + r * 512:
                                            kc * N + (r + 1) * 512],
                                        start=(kc == 0), stop=(kc == 7))
                                he_mm[idx] = tk.mark(m, "T")
                                if ht == 7:
                                    prev_he[0] = he_mm[idx]
                            b.tensor(lambda t, _f=_t, _pf=fv_prev_g[0]:
                                     _f(t, pf_fence=_pf))

                            def _s(s, hi=hi, ht=ht, e=e, idx=idx):
                                tk.wait(s, he_mm[idx])
                                m = s.activation(
                                    he[hi][:, ht * 512:(ht + 1) * 512],
                                    phe[idx][:], AF.Gelu,
                                    bias=c_b1[:, e * 8 + ht: e * 8 + ht + 1])
                                ge_tok[idx] = tk.mark(m, "S")
                            b.scalar(_s)

                        def _gl(g, hi=hi, r=r):
                            en = e1 if hi == 0 else ((3 * (r + 1)) % 8 if r < 7
                                                     else None)
                            if en is None:
                                return
                            tk.wait(g, prev_he[0])
                            w1_tok[0] = d_w.start(g, w1b[:], dtb["w1e"].ap()[en])
                        b.gpsimd(_gl)

                with (nc.psum_tensor("pf%d" % r, [128, 2048], F32) as pf,
                      nc.psum_tensor("pt5_%d" % r, [128, 1024], F32) as pt5):
                    last_ge = (ge_tok[0], ge_tok[1])
                    fv_prev = fv_prev_g
                    for tch in range(2):
                        fdt = [None]

                        def _sf(s, r=r, tch=tch):
                            tk.wait(s, fv_prev[0])
                            src = bass.AP(fsh, r * 512 + tch * 256,
                                          [[N, 128], [128 * N, 8], [1, 256]])
                            fdt[0] = d_f.start(s, fshs[:], src)
                        b.scalar(_sf)
                        w2t = [None, None]
                        ct_mm = [None] * 8
                        for ct in range(8):
                            def _g4(g, ct=ct, e0=e0, e1=e1):
                                tk.wait(g, prev_w2[ct % 2])
                                d_w.start(g, w2s[2 * (ct % 2)][:],
                                          dtb["w2e"].ap()[e0][ct])
                                w2t[ct % 2] = d_w.start(g, w2s[2 * (ct % 2) + 1][:],
                                                        dtb["w2e"].ap()[e1][ct])
                            b.gpsimd(_g4)

                            def _t5(t, ct=ct, tch=tch,
                                    vfence=None):
                                Dma.wait(t, w2t[ct % 2])
                                if ct == 0:
                                    tk.wait(t, last_ge[0])
                                    tk.wait(t, last_ge[1])
                                    tk.wait(t, vfence)
                                m = None
                                for hi in range(2):
                                    for kc in range(8):
                                        m = t.matmul(
                                            pf[:, ct * 256:(ct + 1) * 256],
                                            w2s[2 * (ct % 2) + hi][:,
                                                kc * 128:(kc + 1) * 128],
                                            he[hi][:, kc * 512 + tch * 256:
                                                   kc * 512 + tch * 256 + 256],
                                            start=(hi == 0 and kc == 0),
                                            stop=(hi == 1 and kc == 7))
                                ct_mm[ct] = tk.mark(m, "T")
                                prev_w2[ct % 2] = ct_mm[ct]
                                if ct == 7:
                                    prev_ct[0] = ct_mm[ct]
                            b.tensor(lambda t, _f=_t5, _vf=fv_prev[0]: _f(t, vfence=_vf))

                        fv_tok = [None]

                        def _vf(v, tch=tch):
                            tk.wait(v, ct_mm[7])
                            Dma.wait(v, fdt[0])
                            m = v.scalar_tensor_tensor(fsc[:], pf[:], 0.25,
                                                       fshs[:], ALU.mult, ALU.add)
                            fv_tok[0] = tk.mark(m, "V")
                            fv_prev[0] = fv_tok[0]
                        b.vector(_vf)
                        if _DUMP == "fsc" and r == 0 and tch == 0:
                            def _gd(g):
                                tk.wait(g, fv_tok[0])
                                t = d_o.start(g, bass.AP(out, 0, [[2048, 128], [1, 2048]]), fsc[:])
                                t = d_o.start(g, bass.AP(out, 128 * 2048, [[4096, 128], [1, 4096]]), he[0][:])
                                t = d_o.start(g, bass.AP(out, 128 * 2048 + 128 * 4096, [[4096, 128], [1, 4096]]), he[1][:])
                                t = d_o.start(g, bass.AP(out, 128 * 2048 + 2 * 128 * 4096, [[2048, 128], [1, 2048]]), fshs[:])
                                d_o.wait(g, t)
                            b.gpsimd(_gd)

                        for j in range(2 if _DUMP != "fsc" else 0):
                            rt = r * 4 + tch * 2 + j
                            trt = [None]

                            def _t6(t, j=j):
                                tk.wait(t, fv_tok[0])
                                tk.wait(t, tk.cur("V") if j else fv_tok[0])
                                for ct in range(8):
                                    m = t.transpose(
                                        pt5[:, ct * 128:(ct + 1) * 128],
                                        fsc[:, ct * 256 + j * 128:
                                            ct * 256 + (j + 1) * 128],
                                        c_id32[:])
                                trt[0] = tk.mark(m, "T")
                            b.tensor(_t6)
                            dtok = [None]

                            def _y(sync, rt=rt):
                                tk.wait(sync, tk.cur("V"))
                                dtok[0] = d_x.start(sync, xt0[:], x_resmaj_ap(rt))
                            b.sync(_y)

                            def _v5(v):
                                tk.wait(v, trt[0])
                            b.vector(_v5)
                            sq_tok = emit_stats(b, pt5, 1024, stat[:, 0:1],
                                                stat[:, 1:2], stat[:, 2:3], sqo)
                            rs_tok = emit_rs(b, stat[:, 2:3], stat[:, 3:4], 1024,
                                             sq_tok)
                            vout = [None]

                            def _v6(v, rt=rt):
                                tk.wait(v, rs_tok)
                                Dma.wait(v, dtok[0])
                                Dma.wait(v, prev_out[0])
                                v.tensor_scalar(app[:], pt5[:], stat[:, 0:1],
                                                stat[:, 3:4], ALU.subtract,
                                                ALU.mult)
                                v.tensor_add(app[:], app[:],
                                             ln3r[:, rt * 1024:(rt + 1) * 1024])
                                m = v.tensor_add(app[:], app[:], xt0[:])
                                vout[0] = tk.mark(m, "V")
                            b.vector(_v6)

                            def _y2(sync, rt=rt):
                                tk.wait(sync, vout[0])
                                prev_out[0] = d_o.start(sync, out_resmaj_ap(rt),
                                                        app[:])
                            b.sync(_y2)

                            def _t8(t):
                                tk.wait(t, tk.cur("V"))
                            b.tensor(_t8)

    with nc.Block("fin") as b:
        def _y(sync):
            sync.wait_ge(d_o.sem, d_o.cnt)
        b.sync(_y)

    sU2.close()
    sLN.close()
    st.close()
    nc.finalize()
    return nc


_PROG = None


def _host_prep(inputs):
    f32 = np.float32
    bf = ml_dtypes.bfloat16
    x = np.ascontiguousarray(np.asarray(inputs["L0_attn_x"], f32))
    perm = np.concatenate([h * 64 + np.concatenate([np.arange(0, 64, 2),
                                                    np.arange(1, 64, 2)])
                           for h in range(H)])
    Wq = np.asarray(inputs["L0_attn_Wq"], f32)[perm]
    Wk = np.asarray(inputs["L0_attn_Wk"], f32)[perm]
    Wv = np.asarray(inputs["L0_attn_Wv"], f32)
    Wo = np.asarray(inputs["L0_attn_Wo"], f32)
    def tile_kM(A, Mt):
        K_, M_ = A.shape
        return np.ascontiguousarray(
            A.reshape(K_ // 128, 128, M_ // Mt, Mt).transpose(2, 1, 0, 3)
            .reshape(M_ // Mt, 128, (K_ // 128) * Mt))

    com = {
        "wqt": tile_kM(np.ascontiguousarray(Wq.T), 128),
        "wkt": tile_kM(np.ascontiguousarray(Wk.T), 128),
        "wvt": tile_kM(np.ascontiguousarray(Wv.T), 128),
        "wot": np.ascontiguousarray(
            np.ascontiguousarray(Wo.T).reshape(8, 128, 1024)
            .transpose(1, 0, 2).reshape(128, 8192)),
        "bq_r": np.ascontiguousarray(np.asarray(inputs["L0_attn_bq"], f32)[perm])[None, :],
        "bk_r": np.ascontiguousarray(np.asarray(inputs["L0_attn_bk"], f32)[perm])[None, :],
        "bv_r": np.asarray(inputs["L0_attn_bv"], f32)[None, :].copy(),
        "bo_r": np.asarray(inputs["L0_attn_bo"], f32)[None, :].copy(),
        "g8": np.ascontiguousarray(np.asarray(inputs["L0_attn_ln_w"], f32).reshape(8, 128).T),
        "b8": np.ascontiguousarray(np.asarray(inputs["L0_attn_ln_b"], f32).reshape(8, 128).T),
        "sw1": tile_kM(np.asarray(inputs["L0_ffn_sw1"], f32), 128),
        "sw3": tile_kM(np.asarray(inputs["L0_ffn_sw3"], f32), 128),
        "sw2": tile_kM(np.asarray(inputs["L0_ffn_sw2"], f32), 128),
        "w1e": np.stack([
            np.asarray(inputs["L0_ffn_w1"], f32)[e].reshape(8, 128, 1024)
            .transpose(1, 0, 2).reshape(128, 8192) for e in range(NE)]),
        "w2e": np.stack([tile_kM(np.asarray(inputs["L0_ffn_w2"], f32)[e], 128)
                         for e in range(NE)]),
        "sb1_16": np.ascontiguousarray(np.asarray(inputs["L0_ffn_sb1"], f32).reshape(16, 128).T),
        "sb3_16": np.ascontiguousarray(np.asarray(inputs["L0_ffn_sb3"], f32).reshape(16, 128).T),
        "sb2_8": np.ascontiguousarray(np.asarray(inputs["L0_ffn_sb2"], f32).reshape(8, 128).T),
        "b1_64": np.ascontiguousarray(
            np.asarray(inputs["L0_ffn_b1"], f32).reshape(NE, 8, 128)
            .transpose(2, 0, 1).reshape(128, 64)),
        "b2_64": np.ascontiguousarray(
            np.asarray(inputs["L0_ffn_b2"], f32).reshape(NE, 8, 128)
            .transpose(2, 0, 1).reshape(128, 64)),
        "id16": np.eye(128, dtype=bf),
        "id32": np.eye(128, dtype=f32),
    }
    invf = 10000.0 ** (-np.arange(0, HD, 2, dtype=np.float64) / HD)
    pos = np.arange(NT)[:, None] * 128 + np.arange(128)[None, :]
    ang = pos[:, :, None] * invf[None, None, :]
    com["cosA"] = np.ascontiguousarray(
        np.cos(ang).transpose(1, 0, 2).reshape(128, NT * 32)).astype(bf)
    com["sinA"] = np.ascontiguousarray(
        np.sin(ang).transpose(1, 0, 2).reshape(128, NT * 32)).astype(bf)
    return x, com


def kernel(**inputs):
    global _PROG
    x, com = _host_prep(inputs)
    if _PROG is None:
        _PROG = build_program()
    in_maps = [dict(com, xb=np.ascontiguousarray(x[c])) for c in range(NCORES)]
    res = run_bass_kernel_spmd(_PROG, in_maps, core_ids=list(range(NCORES)))
    return np.stack([np.asarray(r["out"], np.float32) for r in res.results], 0)


# version 11 (final)
# speedup vs baseline: 1.1837x; 1.1837x over previous
import contextlib
import numpy as np
import ml_dtypes
import concourse.bass as bass
import concourse.mybir as mybir
from concourse.bass_utils import run_bass_kernel_spmd

F32 = mybir.dt.float32
BF16 = mybir.dt.bfloat16
AF = mybir.ActivationFunctionType
ALU = mybir.AluOpType
AX = mybir.AxisListType

D, H, HD, N = 1024, 16, 64, 4096
NT = 32
EPS = 1e-5
HS, HR, NE = 2048, 1024, 8
NCORES = 8
OB_W = 8 * N
_UPTO = 5
_PH4STAGE = 3
_NQ = 8
_ZMODE = 4
_NONORM = False
_NORMLVL = 6
_DUMP = None


class Toks:
    ENG = ("T", "S", "V", "G", "Y")

    def __init__(self, nc, stack):
        self.sem = {e: stack.enter_context(nc.semaphore("tok" + e)) for e in self.ENG}
        self.cnt = {e: 0 for e in self.ENG}

    def mark(self, inst, e):
        self.cnt[e] += 1
        inst.then_inc(self.sem[e], 1)
        return (self.sem[e], self.cnt[e])

    def cur(self, e):
        return (self.sem[e], self.cnt[e])

    @staticmethod
    def wait(eng, tok):
        if tok is not None and tok[1] > 0:
            eng.wait_ge(tok[0], tok[1])

    def fence(self, eng, inst, e):
        t = self.mark(inst, e)
        self.wait(eng, t)
        return t


class Dma:
    def __init__(self, nc, stack, name):
        self.sem = stack.enter_context(nc.semaphore("dma_" + name))
        self.cnt = 0

    def start(self, eng, dst, src):
        eng.dma_start(dst, src).then_inc(self.sem, 16)
        self.cnt += 16
        return (self.sem, self.cnt)

    @staticmethod
    def wait(eng, tok):
        if tok is not None:
            eng.wait_ge(tok[0], tok[1])


def build_program():
    nc = bass.Bass("TRN2", target_bir_lowering=False)
    st = contextlib.ExitStack()
    dtb = {}

    def dram(name, shape, dtyp=F32, kind="ExternalInput"):
        dtb[name] = nc.dram_tensor(name, shape, dtyp, kind=kind)
        return dtb[name]

    xb = dram("xb", [N, D])
    out = dram("out", [N, D], F32, "ExternalOutput")
    for nm in ("wqt", "wkt", "wvt"):
        dram(nm, [8, 128, 1024])
    dram("wot", [128, 8 * 1024])
    for nm in ("bq_r", "bk_r", "bv_r", "bo_r"):
        dram(nm, [1, D])
    dram("g8", [128, 8]); dram("b8", [128, 8])
    dram("cosA", [128, 1024], BF16); dram("sinA", [128, 1024], BF16)
    dram("id16", [128, 128], BF16); dram("id32", [128, 128], F32)
    dram("sw1", [16, 128, 1024]); dram("sw3", [16, 128, 1024])
    dram("sw2", [8, 128, 2048])
    dram("w1e", [NE, 128, 8 * 1024]); dram("w2e", [NE, 8, 128, 1024])
    dram("sb1_16", [128, 16]); dram("sb3_16", [128, 16]); dram("sb2_8", [128, 8])
    dram("b1_64", [128, 64]); dram("b2_64", [128, 64])
    fsh = dram("fsh", [D, N], BF16, "Internal")

    tk = Toks(nc, st)
    d_x = Dma(nc, st, "x")
    d_w = Dma(nc, st, "w")
    d_f = Dma(nc, st, "f")
    d_o = Dma(nc, st, "o")

    def SB(stack, name, shape, dtyp, side=None):
        return stack.enter_context(nc.sbuf_tensor(name, shape, dtyp, side=side))

    sWB = contextlib.ExitStack()
    c_id16 = SB(st, "c_id16", [128, 128], BF16)
    c_id32 = SB(st, "c_id32", [128, 128], F32)
    c_g8 = SB(st, "c_g8", [128, 8], F32)
    c_b8 = SB(st, "c_b8", [128, 8], F32)
    c_sb1 = SB(st, "c_sb1", [128, 16], F32)
    c_sb3 = SB(st, "c_sb3", [128, 16], F32)
    c_sb2 = SB(st, "c_sb2", [128, 8], F32)
    c_b1 = SB(st, "c_b1", [128, 64], F32)
    c_b2 = SB(st, "c_b2", [128, 64], F32)
    c_shb = SB(st, "c_shb", [128, 64], F32)
    c_ones1 = SB(st, "c_ones1", [1, 128], BF16)
    c_ones128 = SB(st, "c_ones128", [128, 1], BF16)
    stat = SB(st, "stat", [128, 16], F32)
    c_bor = SB(st, "c_bor", [1, D], BF16)
    c_cos = SB(sWB, "c_cos", [128, 1024], BF16)
    c_sin = SB(sWB, "c_sin", [128, 1024], BF16)
    c_bqr = SB(sWB, "c_bqr", [1, D], BF16)
    c_bkr = SB(sWB, "c_bkr", [1, D], BF16)
    c_bvr = SB(sWB, "c_bvr", [1, D], BF16)

    with nc.Block("consts") as b:
        def _y(sync):
            t = None
            for nm, dst in (("id16", c_id16), ("id32", c_id32), ("cosA", c_cos),
                            ("sinA", c_sin), ("g8", c_g8), ("b8", c_b8),
                            ("sb1_16", c_sb1), ("sb3_16", c_sb3),
                            ("sb2_8", c_sb2), ("b1_64", c_b1), ("b2_64", c_b2)):
                t = d_x.start(sync, dst[:], dtb[nm].ap())
            d_x.wait(sync, t)
        b.sync(_y)

        def _g(g):
            t = None
            for nm, dst in (("bq_r", c_bqr), ("bk_r", c_bkr), ("bv_r", c_bvr),
                            ("bo_r", c_bor)):
                t = d_w.start(g, dst[:], dtb[nm].ap())
            d_w.wait(g, t)
        b.gpsimd(_g)

        def _v(v):
            Dma.wait(v, (d_x.sem, d_x.cnt))
            Dma.wait(v, (d_w.sem, d_w.cnt))
            v.memset(c_ones1[:], 1.0)
            m = v.memset(c_ones128[:], 1.0)
            t = tk.mark(m, "V")
            tk.wait(v, t)
            for r in range(8):
                e0, e1 = (3 * r) % 8, (3 * r + 7) % 8
                sl = c_shb[:, r * 8:(r + 1) * 8]
                m = v.tensor_add(sl, c_b2[:, e0 * 8:(e0 + 1) * 8],
                                 c_b2[:, e1 * 8:(e1 + 1) * 8])
                tk.fence(v, m, "V")
                m = v.tensor_scalar_mul(sl, sl, 0.25)
                tk.fence(v, m, "V")
                v.scalar_tensor_tensor(sl, c_sb2[:], 0.5, sl, ALU.mult, ALU.add)
        b.vector(_v)

    def emit_stats(b, src, width, m_col, nm_col, ss_col, sq_out):
        t1 = [None]

        def _v(v):
            m = v.reduce_sum(m_col, src[:, 0:width], axis=AX.X)
            tk.fence(v, m, "V")
            m = v.tensor_scalar_mul(m_col, m_col, 1.0 / width)
            tk.fence(v, m, "V")
            m = v.tensor_scalar_sub(sq_out[:, 0:width], src[:, 0:width], m_col)
            tk.fence(v, m, "V")
            m = v.scalar_tensor_tensor(sq_out[:, 0:width], sq_out[:, 0:width],
                                       1.0, sq_out[:, 0:width], ALU.mult,
                                       ALU.mult, accum_out=ss_col)
            t1[0] = tk.mark(m, "V")
        b.vector(_v)
        return t1[0]

    def emit_rs(b, ss_col, rs_col, width, wait_tok, combined=False):
        va, vb = stat[:, 12:13], stat[:, 13:14]
        t1 = [None]

        def _v1(v):
            tk.wait(v, wait_tok)
            m = v.tensor_scalar_mul(va, ss_col, 1.0 / width)
            tk.fence(v, m, "V")
            m = v.tensor_scalar_add(vb, va, EPS)
            tk.fence(v, m, "V")
            m = v.reciprocal(rs_col, vb)
            if combined:
                tk.fence(v, m, "V")
                m = v.tensor_mul(va, va, rs_col)
                tk.fence(v, m, "V")
                m = v.tensor_scalar_add(va, va, EPS)
                tk.fence(v, m, "V")
                m = v.reciprocal(vb, va)
                tk.fence(v, m, "V")
                m = v.tensor_mul(rs_col, rs_col, vb)
            t1[0] = tk.mark(m, "V")
        b.vector(_v1)
        tout = [None]

        def _s(s):
            tk.wait(s, t1[0])
            m = s.activation(rs_col, rs_col, AF.Sqrt)
            tout[0] = tk.mark(m, "S")
        b.scalar(_s)
        return tout[0]

    def x_tile_ap(tt):
        return xb.ap()[tt * 128:(tt + 1) * 128, :]

    def x_resmaj_ap(rt):
        r, j0 = rt // 4, (rt % 4) * 128
        return bass.AP(xb, (8 * j0 + r) * D, [[8 * D, 128], [1, D]])

    def out_resmaj_ap(rt):
        r, j0 = rt // 4, (rt % 4) * 128
        return bass.AP(out, (8 * j0 + r) * D, [[8 * D, 128], [1, D]])

    def emit_dump(aps):
        with nc.Block("dump") as bb:
            def _g(g):
                off = 0
                t = None
                for ap, nel in aps:
                    dst = bass.AP(out, off, [[nel // 128, 128], [1, nel // 128]])
                    t = d_o.start(g, dst, ap)
                    off += nel
                d_o.wait(g, t)
            bb.gpsimd(_g)

    w_B = SB(sWB, "w_B", [128, 8 * N], BF16)

    with contextlib.ExitStack() as s1:
        xt0 = SB(s1, "xt0", [128, 1024], F32)
        xt1 = SB(s1, "xt1", [128, 1024], F32)
        appb = SB(s1, "appb1", [128, 1024], BF16)
        sqo = SB(s1, "sqo1", [128, 1024], F32)
        with nc.psum_tensor("p1t", [128, 1024], BF16) as p1t, \
                nc.Block("wbuild") as b:
            apply_tok = [None, None]
            evac_tok = [None]
            for tt in range(NT if _UPTO >= 1 else 0):
                src = xt0 if tt % 2 == 0 else xt1
                dtok = [None]

                def _y(sync):
                    if tt >= 2:
                        tk.wait(sync, apply_tok[tt % 2])
                    dtok[0] = d_x.start(sync, src[:], x_tile_ap(tt))
                b.sync(_y)

                def _v0(v):
                    Dma.wait(v, dtok[0])
                b.vector(_v0)
                sq_tok = emit_stats(b, src, 1024, stat[:, 0:1], stat[:, 1:2],
                                    stat[:, 2:3], sqo)
                rs_tok = emit_rs(b, stat[:, 2:3], stat[:, 3:4], 1024, sq_tok,
                                 combined=True)

                def _v1(v):
                    tk.wait(v, rs_tok)
                    m = v.tensor_scalar(appb[:], src[:], stat[:, 0:1],
                                        stat[:, 3:4], ALU.subtract, ALU.mult)
                    apply_tok[tt % 2] = tk.mark(m, "V")
                b.vector(_v1)
                trt = [None]

                def _t(t):
                    tk.wait(t, apply_tok[tt % 2])
                    tk.wait(t, evac_tok[0])
                    for c in range(8):
                        m = t.transpose(p1t[:, c * 128:(c + 1) * 128],
                                        appb[:, c * 128:(c + 1) * 128], c_id16[:])
                    trt[0] = tk.mark(m, "T")
                b.tensor(_t)

                def _s(s):
                    tk.wait(s, trt[0])
                    for c in range(8):
                        m = s.activation(
                            w_B[:, c * N + tt * 128: c * N + (tt + 1) * 128],
                            p1t[:, c * 128:(c + 1) * 128], AF.Identity,
                            bias=c_b8[:, c:c + 1], scale=c_g8[:, c:c + 1])
                    evac_tok[0] = tk.mark(m, "S")
                b.scalar(_s)

    if _DUMP == "wB":
        emit_dump([(w_B[:], 128 * 8 * N)])

    sOB = contextlib.ExitStack()
    o_B = SB(sOB, "o_B", [128, OB_W], BF16, side="right")

    with contextlib.ExitStack() as s2:
        qraw = SB(s2, "qraw", [128, N], BF16)
        kraw = SB(s2, "kraw", [128, N], BF16)
        q_B = SB(s2, "q_B", [128, N], BF16)
        k_B = SB(s2, "k_B", [128, N], BF16)
        v_A = SB(s2, "v_A", [128, 32 * 128], BF16)
        P0 = SB(s2, "P0", [128, 1024], BF16)
        P1 = SB(s2, "P1", [128, 1024], BF16)
        P2 = SB(s2, "P2", [128, 1024], BF16)
        P3 = SB(s2, "P3", [128, 1024], BF16)
        mr0, mr1, mr2 = P1, P2, P3
        oae = SB(s2, "oae", [128, 512], BF16)
        oas = SB(s2, "oas", [128, 512], BF16)
        rzb = SB(s2, "rzb", [128, 16], F32)
        zsb = SB(s2, "zsb", [128, 512], F32)
        wqs = SB(s2, "wqs", [128, 1024], BF16)
        wks = SB(s2, "wks", [128, 1024], BF16)
        wvs = SB(s2, "wvs", [128, 1024], BF16)
        cb = SB(s2, "cb", [128, 1], F32)
        cbs = SB(s2, "cbs", [1, 4], F32)
        cbb = SB(s2, "cbb", [1, 1], BF16)
        Ps = (P0, P1, P2, P3)

        prev_proj_tok = [None]

        npair = 8 if _UPTO >= 2 else 0
        if _DUMP in ("qkv", "oB"):
            npair = 1
        for pair in range(npair):
            with nc.Block("pair%d" % pair) as b:
                wtok = [None]

                def _g(g):
                    tk.wait(g, prev_proj_tok[0])
                    t = None
                    for nm, dst in (("wqt", wqs), ("wkt", wks), ("wvt", wvs)):
                        t = d_w.start(g, dst[:], dtb[nm].ap()[pair])
                    wtok[0] = t
                b.gpsimd(_g)

                with (nc.psum_tensor("pq%d" % pair, [128, 512], F32) as pq,
                      nc.psum_tensor("pk%d" % pair, [128, 512], F32) as pk,
                      nc.psum_tensor("pv%d" % pair, [128, 512], F32) as pv,
                      nc.psum_tensor("pq2%d" % pair, [128, 512], F32) as pq2,
                      nc.psum_tensor("pk2%d" % pair, [128, 512], F32) as pk2,
                      nc.psum_tensor("pv2%d" % pair, [128, 512], F32) as pv2):
                    pqs, pks, pvs = (pq, pq2), (pk, pk2), (pv, pv2)
                    ev_tok = [None, None]
                    mm_tok = [None, None]
                    for tg in range(8):
                        def _t(t):
                            if tg == 0:
                                Dma.wait(t, wtok[0])
                            if tg >= 2:
                                tk.wait(t, ev_tok[tg % 2])
                            A, Bm, C = pqs[tg % 2], pks[tg % 2], pvs[tg % 2]
                            for j in range(4):
                                tt = tg * 4 + j
                                for kc in range(8):
                                    sw = w_B[:, kc * N + tt * 128:
                                             kc * N + (tt + 1) * 128]
                                    st0 = (kc == 0)
                                    t.matmul(A[:, j * 128:(j + 1) * 128], sw,
                                             wqs[:, kc * 128:(kc + 1) * 128],
                                             start=st0, stop=False)
                                    t.matmul(Bm[:, j * 128:(j + 1) * 128], sw,
                                             wks[:, kc * 128:(kc + 1) * 128],
                                             start=st0, stop=False)
                                    t.matmul(C[:, j * 128:(j + 1) * 128], sw,
                                             wvs[:, kc * 128:(kc + 1) * 128],
                                             start=st0, stop=False)
                                t.matmul(A[:, j * 128:(j + 1) * 128], c_ones1[:],
                                         c_bqr[:, pair * 128:(pair + 1) * 128],
                                         start=False, stop=True)
                                t.matmul(Bm[:, j * 128:(j + 1) * 128], c_ones1[:],
                                         c_bkr[:, pair * 128:(pair + 1) * 128],
                                         start=False, stop=True)
                                m = t.matmul(C[:, j * 128:(j + 1) * 128],
                                             c_ones1[:],
                                             c_bvr[:, pair * 128:(pair + 1) * 128],
                                             start=False, stop=True)
                            mm_tok[tg % 2] = tk.mark(m, "T")
                            if tg == 7:
                                prev_proj_tok[0] = mm_tok[tg % 2]
                        b.tensor(_t)

                        def _v(v):
                            tk.wait(v, mm_tok[tg % 2])
                            A, Bm, C = pqs[tg % 2], pks[tg % 2], pvs[tg % 2]
                            v.tensor_copy(qraw[:, tg * 512:(tg + 1) * 512], A[:])
                            v.tensor_copy(kraw[:, tg * 512:(tg + 1) * 512], Bm[:])
                            m = v.tensor_copy(v_A[:, tg * 512:(tg + 1) * 512],
                                              C[:])
                            ev_tok[tg % 2] = tk.mark(m, "V")
                        b.vector(_v)

                rope_tok = [None]

                def _vo(v):
                    tk.wait(v, ev_tok[0])
                    tk.wait(v, ev_tok[1])
                    for raw in (qraw, kraw):
                        for hf in range(2):
                            o0 = hf * 16 * 128
                            oc = hf * 16 * 32
                            tops = bass.AP(raw, o0, [[N, 128], [128, 16], [64, 2], [1, 32]])
                            bots = bass.AP(raw, o0 + 32, [[N, 128], [128, 16], [64, 2], [1, 32]])
                            cosr = bass.AP(c_cos, oc, [[1024, 128], [32, 16], [0, 2], [1, 32]])
                            sinr = bass.AP(c_sin, oc, [[1024, 128], [32, 16], [0, 2], [1, 32]])
                            m = v.tensor_tensor(mr0[:], tops, cosr, ALU.mult)
                            v.tensor_tensor(mr1[:], bots, sinr, ALU.mult)
                            v.tensor_tensor(mr2[:], tops, sinr, ALU.mult)
                            v.tensor_tensor(tops, mr0[:], mr1[:], ALU.subtract)
                            v.tensor_tensor(mr0[:], bots, cosr, ALU.mult)
                            m = v.tensor_tensor(bots, mr2[:], mr0[:], ALU.add)
                    rope_tok[0] = tk.mark(m, "V")
                b.vector(_vo)

                cb_tok = [None]
                with nc.psum_tensor("p2t%d" % pair, [128, 1024], BF16) as p2t:
                    tb_ev = [None]
                    for raw, dstB in ((qraw, q_B), (kraw, k_B)):
                        for grp in range(4):
                            trt = [None]

                            def _t(t, raw=raw, grp=grp):
                                tk.wait(t, rope_tok[0])
                                tk.wait(t, tb_ev[0])
                                for c in range(8):
                                    tt = grp * 8 + c
                                    m = t.transpose(p2t[:, c * 128:(c + 1) * 128],
                                                    raw[:, tt * 128:(tt + 1) * 128],
                                                    c_id16[:])
                                trt[0] = tk.mark(m, "T")
                            b.tensor(_t)

                            def _v(v, dstB=dstB, grp=grp):
                                tk.wait(v, trt[0])
                                m = v.tensor_copy(
                                    dstB[:, grp * 1024:(grp + 1) * 1024], p2t[:])
                                tb_ev[0] = tk.mark(m, "V")
                            b.vector(_v)

                with nc.psum_tensor("pn%d" % pair, [128, 4096], F32) as pn:
                    sq_tok = [None]

                    def _vsq(v):
                        tk.wait(v, tb_ev[0])
                        v.tensor_mul(qraw[:, 0:2048], q_B[:, 0:2048],
                                     q_B[:, 0:2048])
                        v.tensor_mul(qraw[:, 2048:4096], q_B[:, 2048:4096],
                                     q_B[:, 2048:4096])
                        v.tensor_mul(kraw[:, 0:2048], k_B[:, 0:2048],
                                     k_B[:, 0:2048])
                        m = v.tensor_mul(kraw[:, 2048:4096],
                                         k_B[:, 2048:4096],
                                         k_B[:, 2048:4096])
                        sq_tok[0] = tk.mark(m, "V")
                    b.vector(_vsq)
                    nm_tok = [None]

                    def _tn(t):
                        tk.wait(t, sq_tok[0])
                        for j, buf in enumerate((qraw, kraw)):
                            for ch in range(4):
                                m = t.matmul(
                                    pn[0:1, (j * 4 + ch) * 512:
                                       (j * 4 + ch + 1) * 512],
                                    c_ones128[:],
                                    bass.AP(buf, ch * 1024,
                                            [[4096, 128], [512, 2], [1, 512]])
                                    if False else
                                    buf[:, ch * 1024: ch * 1024 + 512],
                                    start=True, stop=True)
                        for j, buf in enumerate((qraw, kraw)):
                            for ch in range(4):
                                m = t.matmul(
                                    pn[0:1, (j * 4 + ch) * 512:
                                       (j * 4 + ch + 1) * 512],
                                    c_ones128[:],
                                    buf[:, (ch + 4) * 512 + 0:0] if False else
                                    buf[:, 2048 + ch * 512: 2048 + (ch + 1) * 512],
                                    start=False, stop=True)
                        nm_tok[0] = tk.mark(m, "T")
                    b.tensor(_tn)
                    ct3 = [None]

                    def _vn(v):
                        tk.wait(v, nm_tok[0])
                        m = v.reduce_max(cbs[0:1, 0:1], pn[0:1, 0:2048],
                                         axis=AX.X)
                        tk.fence(v, m, "V")
                        m = v.reduce_max(cbs[0:1, 1:2], pn[0:1, 2048:4096],
                                         axis=AX.X)
                        tk.fence(v, m, "V")
                        m = v.tensor_mul(cbs[0:1, 2:3], cbs[0:1, 0:1],
                                         cbs[0:1, 1:2])
                        ct3[0] = tk.mark(m, "V")
                    b.vector(_vn)
                    ct4 = [None]

                    def _sc(s):
                        tk.wait(s, ct3[0])
                        m = s.activation(cbs[0:1, 3:4], cbs[0:1, 2:3], AF.Sqrt)
                        ct4[0] = tk.mark(m, "S")
                    b.scalar(_sc)
                    ct5 = [None]

                    def _vc2(v):
                        tk.wait(v, ct4[0])
                        m = v.tensor_scalar_mul(cbs[0:1, 3:4], cbs[0:1, 3:4],
                                                -0.125)
                        tk.fence(v, m, "V")
                        m = v.tensor_copy(cbb[:], cbs[0:1, 3:4])
                        ct5[0] = tk.mark(m, "V")
                    b.vector(_vc2)
                    ct6 = [None]

                    def _tc(t):
                        tk.wait(t, ct5[0])
                        m = t.matmul(pn[:, 0:1], c_ones1[:], cbb[:],
                                     start=True, stop=True)
                        ct6[0] = tk.mark(m, "T")
                    b.tensor(_tc)

                    def _vc3(v):
                        tk.wait(v, ct6[0])
                        m = v.tensor_copy(cb[:], pn[:, 0:1])
                        cb_tok[0] = tk.mark(m, "V")
                    b.vector(_vc3)

                with (nc.psum_tensor("sc0%d" % pair, [128, 1024], F32) as sc0,
                      nc.psum_tensor("sc1%d" % pair, [128, 1024], F32) as sc1,
                      nc.psum_tensor("pvAB%d" % pair, [128, 512], F32) as pvAB,
                      nc.psum_tensor("zps%d" % pair, [128, 512], F32) as zps,
                      nc.psum_tensor("tpbA%d" % pair, [128, 512], BF16) as tpbA,
                      nc.psum_tensor("tpbB%d" % pair, [128, 512], BF16) as tpbB):
                    scs = (sc0, sc1)
                    semT, semS = tk.sem["T"], tk.sem["S"]
                    ev_t = [None]
                    ft_t = [None]
                    rz_t = [None]
                    scale_tok = [None]
                    bt_t = [None]
                    obe_tok = [None]
                    prev_cnt = [None]

                    def mk_order(Q):
                        O = [("sc", 0), ("sc", 1)] if Q == 0 else []
                        for p in range(16):
                            O.append(("pv", 2 * p))
                            for kt in (2 * p + 2, 2 * p + 3):
                                if kt < 32:
                                    O.append(("sc", kt))
                                elif Q < _NQ - 1:
                                    O.append(("scn", kt - 32))
                                if kt == 2 * p + 2:
                                    O.append(("pv", 2 * p + 1))
                            O.append(("z", p))
                        return O

                    def emit_evac(pvc):
                        def _ve(v):
                            v.wait_ge(semT, pvc[("z", 15)])
                            tk.wait(v, ft_t[0])
                            v.tensor_copy(oae[:, 0:512], pvAB[:])
                            for rr in (0, 32, 64, 96):
                                m = v.tensor_copy(zsb[rr:rr + 1, :],
                                                  zps[rr:rr + 1, :])
                            ev_t[0] = tk.mark(m, "V")
                        b.vector(_ve)
                        if _NORMLVL < 2:
                            return

                        def _tf(t):
                            tk.wait(t, ev_t[0])
                            tk.wait(t, obe_tok[0])
                            m = None
                            if _NORMLVL != 16:
                                js = (range(4) if _NORMLVL == 17 else
                                      range(4, 8) if _NORMLVL == 18 else
                                      range(8))
                                for j in js:
                                    h, jj = (0, j) if j < 4 else (1, j - 4)
                                    src_ap = bass.AP(oae,
                                                     h * 64 * 512 + jj * 128,
                                                     [[512, 64], [1, 128]])
                                    ident = (c_id16[0:64, 0:64] if h == 0
                                             else c_id16[64:128, 64:128])
                                    tdst = tpbA if h == 0 else tpbB
                                    m = t.transpose(
                                        tdst[:, jj * 64:(jj + 1) * 64],
                                        src_ap, ident,
                                        tile_position=(64 * h, 0))
                            if _NORMLVL != 15:
                                for jj in range(4):
                                    m = t.transpose(
                                        bass.AP(pvAB, jj * 128,
                                                [[512, 128], [1, 128]]),
                                        bass.AP(zsb, jj * 128,
                                                [[512, 128], [1, 128]]),
                                        c_id32[:])
                            ft_t[0] = tk.mark(m, "T")
                        b.tensor(_tf)
                        if _NORMLVL < 3 or _NORMLVL >= 15:
                            return

                        def _vr(v):
                            tk.wait(v, ft_t[0])
                            tk.wait(v, scale_tok[0])
                            m = v.tensor_copy(
                                rzb[:, 8:12],
                                bass.AP(pvAB, 0, [[512, 128], [128, 4], [1, 1]]))
                            tk.fence(v, m, "V")
                            m = v.scalar_tensor_tensor(
                                rzb[:, 8:12],
                                bass.AP(pvAB, 64, [[512, 128], [128, 4], [1, 1]]),
                                1.0, rzb[:, 8:12], ALU.mult, ALU.add)
                            tk.fence(v, m, "V")
                            m = v.tensor_copy(
                                rzb[:, 12:16],
                                bass.AP(pvAB, 32, [[512, 128], [128, 4], [1, 1]]))
                            tk.fence(v, m, "V")
                            m = v.scalar_tensor_tensor(
                                rzb[:, 12:16],
                                bass.AP(pvAB, 96, [[512, 128], [128, 4], [1, 1]]),
                                1.0, rzb[:, 12:16], ALU.mult, ALU.add)
                            tk.fence(v, m, "V")
                            v.reciprocal(rzb[:, 0:4], rzb[:, 8:12])
                            m = v.reciprocal(rzb[:, 4:8], rzb[:, 12:16])
                            rz_t[0] = tk.mark(m, "V")
                        b.vector(_vr)

                    def emit_qnorm(Qn):
                        if _NORMLVL < 4:
                            return
                        def _ss(s):
                            tk.wait(s, rz_t[0])
                            tk.wait(s, bt_t[0])
                            for j in range(8):
                                h, jj = (0, j) if j < 4 else (1, j - 4)
                                tsrc = tpbA if h == 0 else tpbB
                                m = s.activation(oas[:, j * 64:(j + 1) * 64],
                                                 tsrc[:, jj * 64:
                                                      (jj + 1) * 64],
                                                 AF.Copy,
                                                 scale=rzb[:, h * 4 + jj:
                                                           h * 4 + jj + 1])
                            scale_tok[0] = tk.mark(m, "S")
                        b.scalar(_ss)
                        if _NORMLVL < 5:
                            return

                        def _tb(t):
                            tk.wait(t, scale_tok[0])
                            for j in range(8):
                                tdst = tpbA if j < 4 else tpbB
                                m = t.transpose(tdst[0:64, (j % 4) * 128:
                                                     (j % 4) * 128 + 128],
                                                oas[:, j * 64:(j + 1) * 64],
                                                c_id16[:])
                            bt_t[0] = tk.mark(m, "T")
                        b.tensor(_tb)
                        if _NORMLVL < 6:
                            return

                        def _vb(v, Qn=Qn, pair=pair):
                            tk.wait(v, bt_t[0])
                            m = None
                            for j in range(8):
                                head = 0 if j < 4 else 1
                                jj = j % 4
                                dst = bass.AP(
                                    o_B,
                                    head * 64 * OB_W + pair * N + Qn * 64 + jj * 16,
                                    [[OB_W, 64], [1, 16], [512, 8]])
                                tsrc = tpbA if j < 4 else tpbB
                                m = v.tensor_copy(
                                    dst,
                                    bass.AP(tsrc, (j % 4) * 128,
                                            [[512, 64], [1, 128]]))
                            obe_tok[0] = tk.mark(m, "V")
                        b.vector(_vb)

                    for Q in range(_NQ):
                        if Q >= 1 and not _NONORM:
                            emit_evac(prev_cnt[0])

                        base_T = tk.cnt["T"]
                        base_S = tk.cnt["S"]
                        ORDER = mk_order(Q)
                        cnt = {}
                        c = base_T
                        for item in ORDER:
                            c += 1
                            cnt[item] = c
                        prev_q_cnt = prev_cnt[0]

                        def exp_cnt(kt, base_S=base_S):
                            return base_S + kt + 1

                        def S_body(s, Q=Q, cnt=cnt, prev_q_cnt=prev_q_cnt):
                            for kt in range(32):
                                if kt < 2 and Q > 0:
                                    s.wait_ge(semT, prev_q_cnt[("scn", kt)])
                                else:
                                    s.wait_ge(semT, cnt[("sc", kt)])
                                if kt == 0 and Q == 0:
                                    tk.wait(s, cb_tok[0])
                                if kt < 4:
                                    if prev_q_cnt is not None:
                                        s.wait_ge(semT,
                                                  prev_q_cnt[("z", 14 + kt // 2)])
                                else:
                                    s.wait_ge(semT, cnt[("z", (kt - 4) // 2)])
                                m = s.activation(Ps[kt % 4][:], scs[kt % 2][:],
                                                 AF.Exp, bias=cb[:], scale=0.125)
                                got = tk.mark(m, "S")
                                assert got == (semS, exp_cnt(kt))
                        b.scalar(S_body)

                        def T_body(t, Q=Q, cnt=cnt):
                            def sc(kt, Qt, item):
                                buf = scs[kt % 2]
                                t.matmul(buf[:, 0:512],
                                         k_B[0:64, kt * 128:(kt + 1) * 128],
                                         q_B[0:64, Qt * 512:(Qt + 1) * 512],
                                         start=True, stop=True,
                                         tile_position=(0, 0))
                                m = t.matmul(buf[:, 512:1024],
                                             k_B[64:128, kt * 128:(kt + 1) * 128],
                                             q_B[64:128, Qt * 512:(Qt + 1) * 512],
                                             start=True, stop=True,
                                             tile_position=(64, 0))
                                got = tk.mark(m, "T")
                                assert got == (semT, cnt[item])

                            def pv(kt):
                                t.wait_ge(semS, exp_cnt(kt))
                                if kt == 0:
                                    tk.wait(t, ev_t[0])
                                    tk.wait(t, rz_t[0])
                                Pb = Ps[kt % 4]
                                t.matmul(pvAB[0:64, :],
                                         v_A[:, kt * 128: kt * 128 + 64],
                                         Pb[:, 0:512], start=(kt == 0),
                                         stop=(kt == 31), tile_position=(0, 0))
                                m = t.matmul(pvAB[64:128, :],
                                             v_A[:, kt * 128 + 64:
                                                 kt * 128 + 128],
                                             Pb[:, 512:1024], start=(kt == 0),
                                             stop=(kt == 31),
                                             tile_position=(0, 64))
                                got = tk.mark(m, "T")
                                assert got == (semT, cnt[("pv", kt)])

                            def zmm(p):
                                st0 = (p == 0)
                                if _ZMODE == 4:
                                    t.matmul(zps[0:1, :], c_ones128[:],
                                             Ps[(2 * p) % 4][:, 0:512],
                                             start=st0, stop=(p == 15),
                                             tile_position=(0, 0))
                                    t.matmul(zps[32:33, :], c_ones128[:],
                                             Ps[(2 * p) % 4][:, 512:1024],
                                             start=st0, stop=(p == 15),
                                             tile_position=(0, 32))
                                    t.matmul(zps[64:65, :], c_ones128[:],
                                             Ps[(2 * p + 1) % 4][:, 0:512],
                                             start=st0, stop=(p == 15),
                                             tile_position=(0, 64))
                                    m = t.matmul(zps[96:97, :], c_ones128[:],
                                                 Ps[(2 * p + 1) % 4][:, 512:1024],
                                                 start=st0, stop=(p == 15),
                                                 tile_position=(0, 96))
                                else:
                                    m = t.matmul(zps[0:1, :], c_ones128[:],
                                                 Ps[(2 * p) % 4][:, 0:512],
                                                 start=st0, stop=(p == 15))
                                got = tk.mark(m, "T")
                                assert got == (semT, cnt[("z", p)])

                            if Q == 0:
                                tk.wait(t, cb_tok[0])
                            for kind, idx in ORDER:
                                if kind == "sc":
                                    sc(idx, Q, ("sc", idx))
                                elif kind == "scn":
                                    sc(idx, Q + 1, ("scn", idx))
                                elif kind == "pv":
                                    pv(idx)
                                else:
                                    zmm(idx)
                        b.tensor(T_body)
                        prev_cnt[0] = cnt

                        if Q >= 1 and not _NONORM:
                            emit_qnorm(Q - 1)

                    if not _NONORM:
                        emit_evac(prev_cnt[0])
                        emit_qnorm(_NQ - 1)

        if _DUMP == "qkv":
            emit_dump([(q_B[:], 128 * N), (k_B[:], 128 * N),
                       (v_A[:], 128 * 32 * 130), (cb[:], 128)])
        if _DUMP == "oB":
            emit_dump([(o_B[:, 0:N], 128 * N)])

    sWB.close()

    sLN = contextlib.ExitStack()
    ln3r = SB(sLN, "ln3r", [128, NT * 1024], BF16)

    with contextlib.ExitStack() as s3:
        wotb = SB(s3, "wotb", [128, 8 * 1024], BF16)
        xt0 = SB(s3, "xt0b", [128, 1024], F32)
        xt1 = SB(s3, "xt1b", [128, 1024], F32)
        app = SB(s3, "app3", [128, 1024], F32)
        sqo = SB(s3, "sqo3", [128, 1024], F32)
        wot_tok = [None]
        with nc.Block("woload") as b:
            def _g(g):
                wot_tok[0] = d_w.start(g, wotb[:], dtb["wot"].ap())
            b.gpsimd(_g)

        with (nc.psum_tensor("rp0", [128, 1024], F32) as rp0,
              nc.psum_tensor("rp1", [128, 1024], F32) as rp1,
              nc.Block("wo") as b):
            rps = (rp0, rp1)
            r_tok = [None, None]
            fin_tok = [None, None]
            for rt in range(NT if (_UPTO >= 3 and _DUMP not in ("wB", "qkv", "oB")) else 0):
                src = xt0 if rt % 2 == 0 else xt1
                dtok = [None]

                def _y(sync):
                    if rt >= 2:
                        tk.wait(sync, fin_tok[rt % 2])
                    dtok[0] = d_x.start(sync, src[:], x_resmaj_ap(rt))
                b.sync(_y)

                def _t(t):
                    if rt == 0:
                        Dma.wait(t, wot_tok[0])
                    if rt >= 2:
                        tk.wait(t, fin_tok[rt % 2])
                    R = rps[rt % 2]
                    m = None
                    for cch in range(2):
                        for kc in range(8):
                            t.matmul(R[:, cch * 512:(cch + 1) * 512],
                                     o_B[:, kc * N + rt * 128:
                                         kc * N + (rt + 1) * 128],
                                     wotb[:, kc * 1024 + cch * 512:
                                          kc * 1024 + (cch + 1) * 512],
                                     start=(kc == 0), stop=False)
                        m = t.matmul(R[:, cch * 512:(cch + 1) * 512], c_ones1[:],
                                     c_bor[:, cch * 512:(cch + 1) * 512],
                                     start=False, stop=True)
                    r_tok[rt % 2] = tk.mark(m, "T")
                b.tensor(_t)

                def _v0(v):
                    Dma.wait(v, dtok[0])
                b.vector(_v0)
                sq_tok = emit_stats(b, src, 1024, stat[:, 0:1], stat[:, 1:2],
                                    stat[:, 2:3], sqo)
                rs_tok = emit_rs(b, stat[:, 2:3], stat[:, 3:4], 1024, sq_tok)

                def _v1(v):
                    tk.wait(v, rs_tok)
                    tk.wait(v, r_tok[rt % 2])
                    v.tensor_scalar(app[:], src[:], stat[:, 0:1], stat[:, 3:4],
                                    ALU.subtract, ALU.mult)
                    m = v.tensor_add(app[:], app[:], rps[rt % 2][:])
                    tk.mark(m, "V")
                b.vector(_v1)
                sq2 = emit_stats(b, app, 1024, stat[:, 4:5], stat[:, 5:6],
                                 stat[:, 6:7], sqo)
                rs2 = emit_rs(b, stat[:, 6:7], stat[:, 7:8], 1024, sq2)

                def _v2(v):
                    tk.wait(v, rs2)
                    m = v.tensor_scalar(ln3r[:, rt * 1024:(rt + 1) * 1024],
                                        app[:], stat[:, 4:5], stat[:, 7:8],
                                        ALU.subtract, ALU.mult)
                    fin_tok[rt % 2] = tk.mark(m, "V")
                b.vector(_v2)

        if _DUMP == "ln3r":
            emit_dump([(ln3r[:], 128 * NT * 1024)])

    sOB.close()

    sU2 = contextlib.ExitStack()
    u2g = SB(sU2, "u2g", [128, 8 * N], BF16, side="right")

    with contextlib.ExitStack() as s4:
        hh = SB(s4, "hh", [128, 16 * 1024], BF16)
        sw1s = [SB(s4, "sw1s%d" % i, [128, 1024], BF16) for i in range(2)]
        sw3s = [SB(s4, "sw3s%d" % i, [128, 1024], BF16) for i in range(2)]
        sw2s = [SB(s4, "sw2s%d" % i, [128, 2048], BF16) for i in range(2)]
        ssl = SB(s4, "ssl", [128, 1024], BF16)
        xt0 = SB(s4, "xt0c", [128, 1024], F32)
        app = SB(s4, "app4", [128, 1024], F32)
        appb = SB(s4, "appb4", [128, 1024], BF16)
        sqo = xt0
        fsht = [SB(s4, "fsht%d" % i, [128, 512], BF16) for i in range(2)]

        nblk4 = 4 if (_UPTO >= 4 and _DUMP not in ("wB", "qkv", "oB", "ln3r")) else 0
        with contextlib.ExitStack() as sp4:
            if nblk4:
                p4t = sp4.enter_context(nc.psum_tensor("p4t", [128, 1024], BF16))
                ph1 = sp4.enter_context(nc.psum_tensor("ph1", [128, 512], F32))
                ph3 = sp4.enter_context(nc.psum_tensor("ph3", [128, 512], F32))
                ph1b = sp4.enter_context(nc.psum_tensor("ph1b", [128, 512], F32))
                ph3b = sp4.enter_context(nc.psum_tensor("ph3b", [128, 512], F32))
                pC0 = sp4.enter_context(nc.psum_tensor("pC0", [128, 512], F32))
                pC1 = sp4.enter_context(nc.psum_tensor("pC1", [128, 512], F32))
            ua_tok = [None]
            ue_tok = [None]
            trt_prev = [None]
            wdt = [None] * 4
            hm_tok = [None, None]
            hm_hist = {}
            gct = [0]
            hv_tok = [None, None]
            prev_cv4 = [None, None]
            cdt = [None] * 4
            cm_tok = [None, None]
            cm_hist = {}
            cct = [0]
            cs_tok = [None, None]
            dfw = [None, None]
            with nc.Block("sh") as b:
              if nblk4:
                def _g0(g, cx=(d_x.sem, d_x.cnt), cw=(d_w.sem, d_w.cnt)):
                    Dma.wait(g, cx)
                    Dma.wait(g, cw)
                b.gpsimd(_g0)

                def _y0(sync, cw=(d_w.sem, d_w.cnt)):
                    Dma.wait(sync, cw)
                b.sync(_y0)
              for blk in range(nblk4):
                if True:
                    for rtl in range(8):
                        rt = blk * 8 + rtl
                        dtok = [None]

                        def _y(sync, rt=rt):
                            tk.wait(sync, ua_tok[0])
                            dtok[0] = d_x.start(sync, xt0[:], x_resmaj_ap(rt))
                        b.sync(_y)
                        x2_tok = [None]

                        def _v0(v, rt=rt):
                            Dma.wait(v, dtok[0])
                            m = v.tensor_add(app[:], xt0[:],
                                             ln3r[:, rt * 1024:(rt + 1) * 1024])
                            x2_tok[0] = tk.mark(m, "V")
                        b.vector(_v0)
                        sq_tok = emit_stats(b, app, 1024, stat[:, 0:1],
                                            stat[:, 1:2], stat[:, 2:3], sqo)
                        rs_tok = emit_rs(b, stat[:, 2:3], stat[:, 3:4], 1024,
                                         sq_tok)

                        def _v1(v):
                            tk.wait(v, rs_tok)
                            tk.wait(v, trt_prev[0])
                            m = v.tensor_scalar(appb[:], app[:], stat[:, 0:1],
                                                stat[:, 3:4], ALU.subtract,
                                                ALU.mult)
                            ua_tok[0] = tk.mark(m, "V")
                        b.vector(_v1)
                        trt = [None]

                        def _t(t):
                            tk.wait(t, ua_tok[0])
                            tk.wait(t, ue_tok[0])
                            for c in range(8):
                                m = t.transpose(p4t[:, c * 128:(c + 1) * 128],
                                                appb[:, c * 128:(c + 1) * 128],
                                                c_id16[:])
                            trt[0] = tk.mark(m, "T")
                            trt_prev[0] = trt[0]
                        b.tensor(_t)

                        def _v2(v, rt=rt):
                            tk.wait(v, trt[0])
                            dst = bass.AP(u2g, rt * 128,
                                          [[8 * N, 128], [N, 8], [1, 128]])
                            m = v.tensor_copy(dst, p4t[:])
                            ue_tok[0] = tk.mark(m, "V")
                        b.vector(_v2)

                if True:
                    h1s, h3s = (ph1, ph1b), (ph3, ph3b)
                    nB = 16 if _PH4STAGE >= 2 else 0
                    semT4, semS4, semV4 = tk.sem["T"], tk.sem["S"], tk.sem["V"]
                    bT4, bV4, bS4 = tk.cnt["T"], tk.cnt["V"], tk.cnt["S"]
                    cT4 = lambda ht, ch: bT4 + 2 * ht + ch + 1
                    cV4 = lambda ht, ch: bV4 + 2 * ht + ch + 1
                    cS4 = lambda ht, ch: bS4 + 2 * ht + ch + 1

                    def S_B(s, nB=nB, pcv=tuple(prev_cv4)):
                        for ht in range(nB):
                            for ch in range(2):
                                s.wait_ge(semT4, cT4(ht, ch))
                                if ht >= 1:
                                    s.wait_ge(semV4, cV4(ht - 1, ch))
                                elif pcv[ch] is not None:
                                    s.wait_ge(*pcv[ch])
                                m = s.activation(ssl[:, ch * 512:(ch + 1) * 512],
                                                 h1s[ch][:], AF.Silu,
                                                 bias=c_sb1[:, ht:ht + 1])
                                got = tk.mark(m, "S")
                                assert got == (semS4, cS4(ht, ch))
                    if nB:
                        b.scalar(S_B)
                    for ht in range(nB):
                        gi = gct[0]
                        gct[0] += 1

                        def _g(g, ht=ht, gi=gi):
                            tk.wait(g, hm_hist.get(gi - 2))
                            d_w.start(g, sw1s[gi % 2][:], dtb["sw1"].ap()[ht])
                            wdt[gi % 4] = d_w.start(g, sw3s[gi % 2][:],
                                                    dtb["sw3"].ap()[ht])
                        b.gpsimd(_g)
                        for ch in range(2):
                            def _t(t, ht=ht, ch=ch, blk=blk, gi=gi):
                                Dma.wait(t, wdt[gi % 4])
                                if ht == 0 and ch == 0:
                                    tk.wait(t, ue_tok[0])
                                tk.wait(t, hv_tok[ch])
                                A, Bm = h1s[ch], h3s[ch]
                                lo = blk * 1024 + ch * 512
                                for kc in range(8):
                                    t.matmul(A[:],
                                             sw1s[gi % 2][:, kc * 128:(kc + 1) * 128],
                                             u2g[:, kc * N + lo: kc * N + lo + 512],
                                             start=(kc == 0), stop=(kc == 7))
                                for kc in range(8):
                                    m = t.matmul(
                                        Bm[:],
                                        sw3s[gi % 2][:, kc * 128:(kc + 1) * 128],
                                        u2g[:, kc * N + lo: kc * N + lo + 512],
                                        start=(kc == 0), stop=(kc == 7))
                                hm_tok[ch] = tk.mark(m, "T")
                                assert hm_tok[ch] == (semT4, cT4(ht, ch))
                                if ch == 1:
                                    hm_hist[gi] = hm_tok[ch]
                            b.tensor(_t)

                            def _v(v, ht=ht, ch=ch):
                                v.wait_ge(semS4, cS4(ht, ch))
                                m = v.scalar_tensor_tensor(
                                    hh[:, ht * 1024 + ch * 512:
                                       ht * 1024 + (ch + 1) * 512],
                                    h3s[ch][:], c_sb3[:, ht:ht + 1],
                                    ssl[:, ch * 512:(ch + 1) * 512],
                                    ALU.add, ALU.mult)
                                hv_tok[ch] = tk.mark(m, "V")
                                assert hv_tok[ch] == (semV4, cV4(ht, ch))
                            b.vector(_v)

                for ch in range(2):
                    prev_cv4[ch] = (semV4, cV4(15, ch))
                if True:
                    pcs = (pC0, pC1)
                    for ct in range(8 if _PH4STAGE >= 3 else 0):
                        ci = cct[0]
                        cct[0] += 1

                        def _g(g, ct=ct, ci=ci):
                            tk.wait(g, cm_hist.get(ci - 2))
                            cdt[ci % 4] = d_w.start(g, sw2s[ci % 2][:],
                                                    dtb["sw2"].ap()[ct])
                        b.gpsimd(_g)
                        for ch in range(2):
                            idx = ch

                            def _t(t, ct=ct, ch=ch, blk=blk, idx=idx, ci=ci):
                                Dma.wait(t, cdt[ci % 4])
                                if ct == 0 and ch == 0:
                                    tk.wait(t, hv_tok[0])
                                    tk.wait(t, hv_tok[1])
                                tk.wait(t, cs_tok[idx])
                                m = None
                                for kc in range(16):
                                    m = t.matmul(
                                        pcs[idx][:],
                                        sw2s[ci % 2][:, kc * 128:(kc + 1) * 128],
                                        hh[:, kc * 1024 + ch * 512:
                                           kc * 1024 + (ch + 1) * 512],
                                        start=(kc == 0), stop=(kc == 15))
                                cm_tok[idx] = tk.mark(m, "T")
                                if ch == 1:
                                    cm_hist[ci] = cm_tok[idx]
                            b.tensor(_t)

                            def _s(s, ct=ct, ch=ch, blk=blk, idx=idx):
                                tk.wait(s, cm_tok[idx])
                                Dma.wait(s, dfw[idx])
                                r = 2 * blk + ch
                                m = s.activation(fsht[idx][:], pcs[idx][:],
                                                 AF.Identity,
                                                 bias=c_shb[:, r * 8 + ct:
                                                            r * 8 + ct + 1],
                                                 scale=0.5)
                                t2 = tk.mark(m, "S")
                                tk.wait(s, t2)
                                dfw[idx] = d_f.start(
                                    s, fsh.ap()[ct * 128:(ct + 1) * 128,
                                                blk * 1024 + ch * 512:
                                                blk * 1024 + (ch + 1) * 512],
                                    fsht[idx][:])
                                cs_tok[idx] = t2
                            b.scalar(_s)

        if _DUMP == "u2g":
            emit_dump([(u2g[:], 128 * 8 * N)])
        if _DUMP == "fsh":
            emit_dump([(fsh.ap(), D * N)])

    with contextlib.ExitStack() as s5:
        w1bs = [SB(s5, "w1b%d" % i, [128, 8 * 1024], BF16) for i in range(2)]
        w2s = [SB(s5, "w2s%d" % i, [128, 1024], BF16) for i in range(4)]
        he = [SB(s5, "he%d" % i, [128, 4096], BF16) for i in range(2)]
        fshs = SB(s5, "fshs", [128, 2048], BF16)
        fsc = SB(s5, "fsc", [128, 2048], BF16)
        xt0 = SB(s5, "xt0d", [128, 1024], F32)
        app = SB(s5, "app5", [128, 1024], F32)
        sqo = app

        w1_toks = [None, None]
        he_hist = {}
        fv_prev_g = [None]

        def e_of(ei):
            return ((3 * (ei // 2)) % 8 if ei % 2 == 0
                    else (3 * (ei // 2) + 7) % 8)
        prev_w2 = [None, None]
        prev_ct = [None]
        prev_out = [None]

        nr5 = 8 if (_UPTO >= 5 and _DUMP in (None, "fsc")) else 0
        with contextlib.ExitStack() as sp5:
            if nr5:
                phe0 = sp5.enter_context(nc.psum_tensor("phe0", [128, 512], F32))
                phe1 = sp5.enter_context(nc.psum_tensor("phe1", [128, 512], F32))
                pf = sp5.enter_context(nc.psum_tensor("pf5", [128, 2048], F32))
                pt5 = sp5.enter_context(nc.psum_tensor("pt5", [128, 1024], BF16))
            v6_prev = [None]
            with nc.Block("moe") as b:
              if nr5:
                def _g5(g, cf=(d_f.sem, d_f.cnt), cx=(d_x.sem, d_x.cnt)):
                    Dma.wait(g, cf)
                    Dma.wait(g, cx)
                b.gpsimd(_g5)

                def _y5(sync, cf=(d_f.sem, d_f.cnt), cx=(d_x.sem, d_x.cnt)):
                    Dma.wait(sync, cf)
                    Dma.wait(sync, cx)
                b.sync(_y5)
              for r in range(nr5):
                e0, e1 = (3 * r) % 8, (3 * r + 7) % 8
                if True:
                    phe = (phe0, phe1)
                    he_mm = [None, None]
                    semT5, semS5 = tk.sem["T"], tk.sem["S"]
                    bT5, bS5 = tk.cnt["T"], tk.cnt["S"]
                    cT5 = lambda k: bT5 + k + 1
                    cS5 = lambda k: bS5 + k + 1

                    def S_G(s, e0=e0, e1=e1):
                        for k in range(16):
                            hi, ht = divmod(k, 8)
                            e = e0 if hi == 0 else e1
                            s.wait_ge(semT5, cT5(k))
                            m = s.activation(
                                he[hi][:, ht * 512:(ht + 1) * 512],
                                phe[k % 2][:], AF.Gelu,
                                bias=c_b1[:, e * 8 + ht: e * 8 + ht + 1])
                            got = tk.mark(m, "S")
                            assert got == (semS5, cS5(k))
                    b.scalar(S_G)
                    ge_last = (semS5, cS5(15))
                    if r == 0:
                        def _gp(g):
                            w1_toks[0] = d_w.start(g, w1bs[0][:],
                                                   dtb["w1e"].ap()[e0])
                        b.gpsimd(_gp)
                    for hi, e in enumerate((e0, e1)):
                        ei = 2 * r + hi
                        if ei + 1 < 16:
                            def _gl(g, ei=ei):
                                tk.wait(g, he_hist.get(ei - 1))
                                w1_toks[(ei + 1) % 2] = d_w.start(
                                    g, w1bs[(ei + 1) % 2][:],
                                    dtb["w1e"].ap()[e_of(ei + 1)])
                            b.gpsimd(_gl)
                        for ht in range(8):
                            idx = ht % 2
                            k = hi * 8 + ht

                            def _t(t, hi=hi, ht=ht, r=r, idx=idx, k=k, ei=ei,
                                   pf_fence=None):
                                if ht == 0:
                                    Dma.wait(t, w1_toks[ei % 2])
                                if hi == 0 and ht == 0:
                                    tk.wait(t, prev_ct[0])
                                    tk.wait(t, pf_fence)
                                if k >= 2:
                                    t.wait_ge(semS5, cS5(k - 2))
                                m = None
                                for kc in range(8):
                                    m = t.matmul(
                                        phe[idx][:],
                                        w1bs[ei % 2][:, kc * 1024 + ht * 128:
                                                      kc * 1024 + (ht + 1) * 128],
                                        u2g[:, kc * N + r * 512:
                                            kc * N + (r + 1) * 512],
                                        start=(kc == 0), stop=(kc == 7))
                                he_mm[idx] = tk.mark(m, "T")
                                assert he_mm[idx] == (semT5, cT5(k))
                                if ht == 7:
                                    he_hist[ei] = he_mm[idx]
                            b.tensor(lambda t, _f=_t, _pf=fv_prev_g[0]:
                                     _f(t, pf_fence=_pf))

                if True:
                    last_ge = ge_last
                    fv_prev = fv_prev_g
                    for tch in range(2):
                        fdt = [None]

                        def _sf(s, r=r, tch=tch):
                            tk.wait(s, fv_prev[0])
                            src = bass.AP(fsh, r * 512 + tch * 256,
                                          [[N, 128], [128 * N, 8], [1, 256]])
                            fdt[0] = d_f.start(s, fshs[:], src)
                        b.scalar(_sf)
                        w2t = [None, None]
                        ct_mm = [None] * 8
                        for ct in range(8):
                            def _g4(g, ct=ct, e0=e0, e1=e1):
                                tk.wait(g, prev_w2[ct % 2])
                                d_w.start(g, w2s[2 * (ct % 2)][:],
                                          dtb["w2e"].ap()[e0][ct])
                                w2t[ct % 2] = d_w.start(g, w2s[2 * (ct % 2) + 1][:],
                                                        dtb["w2e"].ap()[e1][ct])
                            b.gpsimd(_g4)

                            def _t5(t, ct=ct, tch=tch,
                                    vfence=None):
                                Dma.wait(t, w2t[ct % 2])
                                if ct == 0:
                                    tk.wait(t, last_ge)
                                    tk.wait(t, vfence)
                                m = None
                                for hi in range(2):
                                    for kc in range(8):
                                        m = t.matmul(
                                            pf[:, ct * 256:(ct + 1) * 256],
                                            w2s[2 * (ct % 2) + hi][:,
                                                kc * 128:(kc + 1) * 128],
                                            he[hi][:, kc * 512 + tch * 256:
                                                   kc * 512 + tch * 256 + 256],
                                            start=(hi == 0 and kc == 0),
                                            stop=(hi == 1 and kc == 7))
                                ct_mm[ct] = tk.mark(m, "T")
                                prev_w2[ct % 2] = ct_mm[ct]
                                if ct == 7:
                                    prev_ct[0] = ct_mm[ct]
                            b.tensor(lambda t, _f=_t5, _vf=fv_prev[0]: _f(t, vfence=_vf))

                        fv_tok = [None]

                        def _vf(v, tch=tch):
                            tk.wait(v, ct_mm[7])
                            Dma.wait(v, fdt[0])
                            m = v.scalar_tensor_tensor(fsc[:], pf[:], 0.25,
                                                       fshs[:], ALU.mult, ALU.add)
                            fv_tok[0] = tk.mark(m, "V")
                            fv_prev[0] = fv_tok[0]
                        b.vector(_vf)
                        if _DUMP == "fsc" and r == 0 and tch == 0:
                            def _gd(g):
                                tk.wait(g, fv_tok[0])
                                t = d_o.start(g, bass.AP(out, 0, [[2048, 128], [1, 2048]]), fsc[:])
                                t = d_o.start(g, bass.AP(out, 128 * 2048, [[4096, 128], [1, 4096]]), he[0][:])
                                t = d_o.start(g, bass.AP(out, 128 * 2048 + 128 * 4096, [[4096, 128], [1, 4096]]), he[1][:])
                                t = d_o.start(g, bass.AP(out, 128 * 2048 + 2 * 128 * 4096, [[2048, 128], [1, 2048]]), fshs[:])
                                d_o.wait(g, t)
                            b.gpsimd(_gd)

                        for j in range(2 if _DUMP != "fsc" else 0):
                            rt = r * 4 + tch * 2 + j
                            trt = [None]

                            def _t6(t, j=j):
                                tk.wait(t, fv_tok[0])
                                tk.wait(t, v6_prev[0])
                                for ct in range(8):
                                    m = t.transpose(
                                        pt5[:, ct * 128:(ct + 1) * 128],
                                        fsc[:, ct * 256 + j * 128:
                                            ct * 256 + (j + 1) * 128],
                                        c_id16[:])
                                trt[0] = tk.mark(m, "T")
                            b.tensor(_t6)
                            dtok = [None]

                            def _y(sync, rt=rt):
                                tk.wait(sync, tk.cur("V"))
                                dtok[0] = d_x.start(sync, xt0[:], x_resmaj_ap(rt))
                            b.sync(_y)

                            def _v5(v):
                                tk.wait(v, trt[0])
                            b.vector(_v5)
                            sq_tok = emit_stats(b, pt5, 1024, stat[:, 0:1],
                                                stat[:, 1:2], stat[:, 2:3], sqo)
                            rs_tok = emit_rs(b, stat[:, 2:3], stat[:, 3:4], 1024,
                                             sq_tok)
                            vout = [None]

                            def _v6(v, rt=rt):
                                tk.wait(v, rs_tok)
                                Dma.wait(v, dtok[0])
                                Dma.wait(v, prev_out[0])
                                v.tensor_scalar(app[:], pt5[:], stat[:, 0:1],
                                                stat[:, 3:4], ALU.subtract,
                                                ALU.mult)
                                v.tensor_add(app[:], app[:],
                                             ln3r[:, rt * 1024:(rt + 1) * 1024])
                                m = v.tensor_add(app[:], app[:], xt0[:])
                                vout[0] = tk.mark(m, "V")
                                v6_prev[0] = vout[0]
                            b.vector(_v6)

                            def _y2(sync, rt=rt):
                                tk.wait(sync, vout[0])
                                prev_out[0] = d_o.start(sync, out_resmaj_ap(rt),
                                                        app[:])
                            b.sync(_y2)


    with nc.Block("fin") as b:
        def _y(sync):
            sync.wait_ge(d_o.sem, d_o.cnt)
        b.sync(_y)

    sU2.close()
    sLN.close()
    st.close()
    nc.finalize()
    return nc


_PROG = None


def _host_prep(inputs):
    f32 = np.float32
    bf = ml_dtypes.bfloat16
    x = np.ascontiguousarray(np.asarray(inputs["L0_attn_x"], f32))
    perm = np.concatenate([h * 64 + np.concatenate([np.arange(0, 64, 2),
                                                    np.arange(1, 64, 2)])
                           for h in range(H)])
    Wq = np.asarray(inputs["L0_attn_Wq"], f32)[perm]
    Wk = np.asarray(inputs["L0_attn_Wk"], f32)[perm]
    Wv = np.asarray(inputs["L0_attn_Wv"], f32)
    Wo = np.asarray(inputs["L0_attn_Wo"], f32)
    def tile_kM(A, Mt):
        K_, M_ = A.shape
        return np.ascontiguousarray(
            A.reshape(K_ // 128, 128, M_ // Mt, Mt).transpose(2, 1, 0, 3)
            .reshape(M_ // Mt, 128, (K_ // 128) * Mt))

    com = {
        "wqt": tile_kM(np.ascontiguousarray(Wq.T), 128),
        "wkt": tile_kM(np.ascontiguousarray(Wk.T), 128),
        "wvt": tile_kM(np.ascontiguousarray(Wv.T), 128),
        "wot": np.ascontiguousarray(
            np.ascontiguousarray(Wo.T).reshape(8, 128, 1024)
            .transpose(1, 0, 2).reshape(128, 8192)),
        "bq_r": np.ascontiguousarray(np.asarray(inputs["L0_attn_bq"], f32)[perm])[None, :],
        "bk_r": np.ascontiguousarray(np.asarray(inputs["L0_attn_bk"], f32)[perm])[None, :],
        "bv_r": np.asarray(inputs["L0_attn_bv"], f32)[None, :].copy(),
        "bo_r": np.asarray(inputs["L0_attn_bo"], f32)[None, :].copy(),
        "g8": np.ascontiguousarray(np.asarray(inputs["L0_attn_ln_w"], f32).reshape(8, 128).T),
        "b8": np.ascontiguousarray(np.asarray(inputs["L0_attn_ln_b"], f32).reshape(8, 128).T),
        "sw1": tile_kM(np.asarray(inputs["L0_ffn_sw1"], f32), 128),
        "sw3": tile_kM(np.asarray(inputs["L0_ffn_sw3"], f32), 128),
        "sw2": tile_kM(np.asarray(inputs["L0_ffn_sw2"], f32), 128),
        "w1e": np.stack([
            np.asarray(inputs["L0_ffn_w1"], f32)[e].reshape(8, 128, 1024)
            .transpose(1, 0, 2).reshape(128, 8192) for e in range(NE)]),
        "w2e": np.stack([tile_kM(np.asarray(inputs["L0_ffn_w2"], f32)[e], 128)
                         for e in range(NE)]),
        "sb1_16": np.ascontiguousarray(np.asarray(inputs["L0_ffn_sb1"], f32).reshape(16, 128).T),
        "sb3_16": np.ascontiguousarray(np.asarray(inputs["L0_ffn_sb3"], f32).reshape(16, 128).T),
        "sb2_8": np.ascontiguousarray(np.asarray(inputs["L0_ffn_sb2"], f32).reshape(8, 128).T),
        "b1_64": np.ascontiguousarray(
            np.asarray(inputs["L0_ffn_b1"], f32).reshape(NE, 8, 128)
            .transpose(2, 0, 1).reshape(128, 64)),
        "b2_64": np.ascontiguousarray(
            np.asarray(inputs["L0_ffn_b2"], f32).reshape(NE, 8, 128)
            .transpose(2, 0, 1).reshape(128, 64)),
        "id16": np.eye(128, dtype=bf),
        "id32": np.eye(128, dtype=f32),
    }
    invf = 10000.0 ** (-np.arange(0, HD, 2, dtype=np.float64) / HD)
    pos = np.arange(NT)[:, None] * 128 + np.arange(128)[None, :]
    ang = pos[:, :, None] * invf[None, None, :]
    com["cosA"] = np.ascontiguousarray(
        np.cos(ang).transpose(1, 0, 2).reshape(128, NT * 32)).astype(bf)
    com["sinA"] = np.ascontiguousarray(
        np.sin(ang).transpose(1, 0, 2).reshape(128, NT * 32)).astype(bf)
    return x, com


def kernel(**inputs):
    global _PROG
    x, com = _host_prep(inputs)
    if _PROG is None:
        _PROG = build_program()
    in_maps = [dict(com, xb=np.ascontiguousarray(x[c])) for c in range(NCORES)]
    res = run_bass_kernel_spmd(_PROG, in_maps, core_ids=list(range(NCORES)))
    return np.stack([np.asarray(r["out"], np.float32) for r in res.results], 0)
